# version 01
# baseline (speedup 1.0000x reference)
import contextlib
import numpy as np

import concourse.bass as bass
import concourse.mybir as mybir
from concourse.bass_utils import run_bass_kernel_spmd

F32 = mybir.dt.float32
AF = mybir.ActivationFunctionType
ALU = mybir.AluOpType
AX = mybir.AxisListType

NCORES = 8
C = 1024
G = 8
E = 8
DFF = 2048
EPS = 1e-5
BIG = 1e9


class Sched:

    ENGS = ("sync", "tensor", "vector", "scalar", "gpsimd")

    def __init__(self, nc, es):
        self.nc = nc
        self.es = es
        self.items = {e: [] for e in self.ENGS}
        self.cnt = {}
        self.sems = {}

    def _sem(self, name):
        if name not in self.sems:
            self.sems[name] = self.es.enter_context(self.nc.semaphore(name))
            self.cnt[name] = 0
        return self.sems[name]

    def op(self, eng, fn, deps=(), inc=1, sem=None):
        if sem is None:
            sem = "e_" + eng
        self._sem(sem)
        waits = {}
        for d in deps:
            if d is None:
                continue
            s, v = d
            self._sem(s)
            if waits.get(s, 0) < v:
                waits[s] = v
        if inc:
            self.cnt[sem] += inc
        self.items[eng].append((fn, waits, sem if inc else None, inc))
        return (sem, self.cnt[sem]) if inc else None

    def tok(self, eng):
        sem = "e_" + eng
        self._sem(sem)
        return (sem, self.cnt[sem])

    def dma(self, eng, sem, out, in_, deps=()):
        def fn(q, out=out, in_=in_):
            return q.dma_start(out=out, in_=in_)
        self._sem(sem)
        deps = list(deps) + [(sem, self.cnt[sem])]
        return self.op(eng, fn, deps=deps, inc=16, sem=sem)

    def fence(self, eng, deps):
        return self.op(eng, lambda q: None, deps=deps, inc=0)

    def build(self, block):
        for eng in self.ENGS:
            items = self.items[eng]
            if not items:
                continue

            def body(q, items=items):
                vclock = {}
                for fn, waits, sem, inc in items:
                    for s in sorted(waits):
                        v = waits[s]
                        if vclock.get(s, 0) < v:
                            q.wait_ge(self.sems[s], v)
                            vclock[s] = v
                    inst = fn(q)
                    if inst is not None and sem is not None:
                        inst.then_inc(self.sems[sem], inc)

            getattr(block, eng)(body)


def build_program(tok=4096, tc=512):
    CH = tok // tc
    NT = tc // 128
    assert NT == 4 and tok % tc == 0

    nc = bass.Bass("TRN2", target_bir_lowering=False, debug=False)

    x_d = nc.dram_tensor("x", [tok, C], F32, kind="ExternalInput").ap()
    wiT_d = nc.dram_tensor("wiT", [C, C], F32, kind="ExternalInput").ap()
    wrT_d = nc.dram_tensor("wrT", [C, C], F32, kind="ExternalInput").ap()
    bi_d = nc.dram_tensor("bi", [128, G], F32, kind="ExternalInput").ap()
    br_d = nc.dram_tensor("br", [128, G], F32, kind="ExternalInput").ap()
    lam_d = nc.dram_tensor("lam", [128, 1], F32, kind="ExternalInput").ap()
    rw_d = nc.dram_tensor("rw", [C, E], F32, kind="ExternalInput").ap()
    up_d = nc.dram_tensor("up", [E, C, DFF], F32, kind="ExternalInput").ap()
    dn_d = nc.dram_tensor("dn", [E, C, C], F32, kind="ExternalInput").ap()
    csel_d = nc.dram_tensor("csel", [128, NCORES], F32, kind="ExternalInput").ap()
    out_d = nc.dram_tensor("out", [tok, C], F32, kind="ExternalOutput").ap()

    sp_at = nc.dram_tensor("sp_at", [CH, G, 128, tc], F32, kind="Internal").ap()
    sp_gx = nc.dram_tensor("sp_gx", [CH, G, 128, tc], F32, kind="Internal").ap()
    agin = nc.dram_tensor("agin", [2, 128, G], F32, kind="Internal")
    agout = nc.dram_tensor("agout", [NCORES, 2, 128, G], F32, kind="Internal")

    x_v = x_d.rearrange("(q u p) c -> q p u c", p=128, u=NT)
    out_v = out_d.rearrange("(q u p) c -> q p u c", p=128, u=NT)
    spat_v = sp_at.rearrange("q g p t -> q p g t")
    spgx_v = sp_gx.rearrange("q g p t -> q p g t")
    wiT_v = wiT_d.rearrange("(cg p) d -> p cg d", p=128)
    wrT_v = wrT_d.rearrange("(cg p) d -> p cg d", p=128)
    rw_v = rw_d.rearrange("(cg p) e -> p cg e", p=128)
    up_v = up_d.rearrange("e (cg p) (s d2) -> e p cg s d2", p=128, d2=512)
    dn_v = dn_d.rearrange("e (dg p) h -> e p dg h", p=128)
    agout_v = agout.ap().rearrange("r t p g -> p r t g")

    es = contextlib.ExitStack()
    with es:
        bx = nc.alloc_sbuf_tensor("bx", [128, NT * C], F32)
        bxf = nc.alloc_sbuf_tensor("bxf", [128, G * tc], F32)
        bit = nc.alloc_sbuf_tensor("bit", [128, G * tc], F32)
        bat = nc.alloc_sbuf_tensor("bat", [128, G * tc], F32)
        bgx = nc.alloc_sbuf_tensor("bgx", [128, G * tc], F32)
        bacc = nc.alloc_sbuf_tensor("bacc", [128, NT * C], F32)
        bht = nc.alloc_sbuf_tensor("bht", [128, NT * C], F32)
        wUa = nc.alloc_sbuf_tensor("wUa", [128, G * C], F32)
        wUb = nc.alloc_sbuf_tensor("wUb", [128, G * C], F32)
        wD = nc.alloc_sbuf_tensor("wD", [128, 4 * C], F32)
        bident = nc.alloc_sbuf_tensor("bident", [128, 128], F32)
        bbi = nc.alloc_sbuf_tensor("bbi", [128, G], F32)
        bbr = nc.alloc_sbuf_tensor("bbr", [128, G], F32)
        brw = nc.alloc_sbuf_tensor("brw", [128, G * E], F32)
        blam = nc.alloc_sbuf_tensor("blam", [128, 1], F32)
        beps = nc.alloc_sbuf_tensor("beps", [128, 1], F32)
        bone = nc.alloc_sbuf_tensor("bone", [128, 1], F32)
        bcsel = nc.alloc_sbuf_tensor("bcsel", [128, NCORES], F32)
        bscrL = nc.alloc_sbuf_tensor("bscrL", [128, tc], F32)
        bscrP = bscrL
        bsilu = nc.alloc_sbuf_tensor("bsilu", [128, 512], F32)
        bhid = nc.alloc_sbuf_tensor("bhid", [128, 512], F32)
        bcarL = nc.alloc_sbuf_tensor("bcarL", [128, G], F32)
        bcarP = nc.alloc_sbuf_tensor("bcarP", [128, G], F32)
        bag = nc.alloc_sbuf_tensor("bag", [128, NCORES * 2 * G], F32)
        bcand = nc.alloc_sbuf_tensor("bcand", [128, NCORES * G], F32)
        bhp = nc.alloc_sbuf_tensor("bhp", [128, 2 * G], F32)
        bHrun = nc.alloc_sbuf_tensor("bHrun", [128, G], F32)
        btmp8 = nc.alloc_sbuf_tensor("btmp8", [128, G], F32)
        bsum = nc.alloc_sbuf_tensor("bsum", [128, NT], F32)
        bssq = nc.alloc_sbuf_tensor("bssq", [128, NT], F32)
        bnm = nc.alloc_sbuf_tensor("bnm", [128, NT], F32)
        bm2t = nc.alloc_sbuf_tensor("bm2t", [128, NT], F32)
        bvar = nc.alloc_sbuf_tensor("bvar", [128, NT], F32)
        bstd = nc.alloc_sbuf_tensor("bstd", [128, NT], F32)
        brstd = nc.alloc_sbuf_tensor("brstd", [128, NT], F32)
        bnm1 = nc.alloc_sbuf_tensor("bnm1", [128, NT], F32)
        bnm2 = nc.alloc_sbuf_tensor("bnm2", [128, NT], F32)
        bgs = nc.alloc_sbuf_tensor("bgs", [128, NT], F32)
        bgr = nc.alloc_sbuf_tensor("bgr", [128, NT], F32)
        bge = nc.alloc_sbuf_tensor("bge", [128, NT * E], F32)
        bmsk = nc.alloc_sbuf_tensor("bmsk", [128, NT * E], F32)
        bsel = nc.alloc_sbuf_tensor("bsel", [128, NT * E], F32)
        bgate = nc.alloc_sbuf_tensor("bgate", [128, NT * E], F32)

        pb = [nc.alloc_psum_tensor("pb%d" % i, [128, 512], F32) for i in range(8)]

        S = Sched(nc, es)
        block = es.enter_context(nc.Block())

        bx3 = bx.ap().rearrange("p (u c) -> p u c", c=C)
        bacc3 = bacc.ap().rearrange("p (u c) -> p u c", c=C)
        bat3 = bat.ap().rearrange("p (g t) -> p g t", t=tc)
        bgx3 = bgx.ap().rearrange("p (g t) -> p g t", t=tc)
        bxf3 = bxf.ap().rearrange("p (g t) -> p g t", t=tc)
        bit_uc = bit.ap().rearrange("p (u c) -> p u c", c=C)
        bht_uc = bht.ap().rearrange("p (u c) -> p u c", c=C)

        t_wi = S.dma("sync", "d_w", wUa.ap().rearrange("p (g c) -> p g c", c=C), wiT_v)
        t_wr = S.dma("sync", "d_w", wUb.ap().rearrange("p (g c) -> p g c", c=C), wrT_v)
        t_bi = S.dma("sync", "d_w", bbi[:, :], bi_d)
        t_br = S.dma("sync", "d_w", bbr[:, :], br_d)
        t_rw = S.dma("sync", "d_w", brw.ap().rearrange("p (g e) -> p g e", e=E), rw_v)
        t_lam = S.dma("sync", "d_w", blam[:, :], lam_d)
        t_cs = S.dma("sync", "d_w", bcsel[:, :], csel_d)

        t_id0 = S.op("gpsimd", lambda q: q.memset(bident[:, :], 1.0))
        t_eps = S.op("vector", lambda q: q.memset(beps[:, :], EPS))
        t_one = S.op("vector", lambda q: q.memset(bone[:, :], 1.0))
        t_id = S.op("gpsimd", lambda q: q.affine_select(
            bident[:, :], bident[:, :], pattern=[[1, 128]],
            compare_op=ALU.is_equal, fill=0.0, channel_multiplier=-1),
            deps=[t_id0])

        t_c1 = S.op("scalar", lambda q: q.activation(blam[:, :], blam[:, :], AF.Exp),
                    deps=[t_lam])
        t_c2 = S.op("scalar", lambda q: q.activation(blam[:, :], blam[:, :], AF.Ln,
                                                     bias=bone[:, 0:1]),
                    deps=[t_c1, t_one])
        t_c = S.op("scalar", lambda q: q.mul(blam[:, :], blam[:, :], -8.0), deps=[t_c2])

        def ln_block(src_ap, src3, sq_dst, sq3, norm_dst, deps_in, extra_norm_deps):
            t_sq = S.op("scalar", lambda q, a=sq_dst, b=src_ap:
                        q.activation(a, b, AF.Square), deps=deps_in)
            t_sums = S.op("vector", lambda q, i=src3: q.reduce_sum(
                bsum[:, :], i, axis=AX.X), deps=deps_in)
            t_ssq = S.op("vector", lambda q, i=sq3: q.reduce_sum(
                bssq[:, :], i, axis=AX.X), deps=[t_sq])
            t_nm = S.op("vector", lambda q: q.tensor_scalar(
                bnm[:, :], bsum[:, :], -1.0 / C, None, op0=ALU.mult), deps=[t_sums])
            t_m2 = S.op("vector", lambda q: q.tensor_tensor(
                bm2t[:, :], bnm[:, :], bnm[:, :], op=ALU.mult), deps=[t_nm])
            t_var = S.op("vector", lambda q: q.scalar_tensor_tensor(
                bvar[:, :], bssq[:, :], 1.0 / C, bm2t[:, :],
                op0=ALU.mult, op1=ALU.subtract), deps=[t_ssq, t_m2])
            t_std = S.op("scalar", lambda q: q.activation(
                bstd[:, :], bvar[:, :], AF.Sqrt, bias=beps[:, 0:1]),
                deps=[t_var, t_eps])
            t_rstd = S.op("vector", lambda q: q.reciprocal(
                brstd[:, :], bstd[:, :]), deps=[t_std])
            t_norm = []
            for u in range(NT):
                t_norm.append(S.op("vector", lambda q, u=u, s=src_ap.tensor,
                                   d=norm_dst: q.tensor_scalar(
                                       d[:, u * C:(u + 1) * C],
                                       s[:, u * C:(u + 1) * C],
                                       bnm[:, u:u + 1], brstd[:, u:u + 1],
                                       op0=ALU.add, op1=ALU.mult),
                                   deps=[t_rstd] + list(extra_norm_deps)))
            return t_norm, t_ssq

        def tmfm_transpose(src, dst3, t_src_ready, dst_war):
            t_cp = [[None] * 2 for _ in range(NT)]
            for u in range(NT):
                for hb in range(2):
                    bank = pb[hb]
                    for k in range(4):
                        g = hb * 4 + k
                        S.op("tensor", lambda q, u=u, g=g, k=k, bank=bank, s=src:
                             q.transpose(bank[:, k * 128:(k + 1) * 128],
                                         s[:, u * C + g * 128:u * C + (g + 1) * 128],
                                         bident[:, :]),
                             deps=([t_src_ready[u], t_id, t_evac.get(id(bank))]
                                   if k == 0 else []),
                             inc=(1 if k == 3 else 0))
                    t_tp = S.tok("tensor")
                    d = dst3[:, hb * 4:hb * 4 + 4, u * 128:(u + 1) * 128]
                    c = S.op("vector", lambda q, d=d, bank=bank: q.tensor_copy(
                        d, bank.ap().rearrange("p (k t) -> p k t", t=128)),
                        deps=[t_tp] + list(dst_war))
                    t_evac[id(bank)] = c
                    t_cp[u][hb] = c
            return t_cp

        t_evac = {}
        t_bx_free = None
        t_bit_r_free = None
        t_bxf_r_free = None
        t_bgx_r_free = None
        t_sp_at_prev = None
        t_sp_gx_prev = None
        t_carL = [None] * G
        t_carP = [None] * G
        t_scr_free = None

        for q in range(CH):
            t_lx = S.dma("sync", "d_x", bx3, x_v[q], deps=[t_bx_free])
            t_norm, t_ssq = ln_block(
                bx.ap(), bx3, bit.ap()[:, :NT * C], bit_uc, bx.ap(),
                deps_in=[t_lx, t_bit_r_free], extra_norm_deps=[])
            t_fmcp = tmfm_transpose(bx.ap(), bxf3, t_norm, [t_bxf_r_free])
            t_fm_all = [t_fmcp[u][hb] for u in range(NT) for hb in range(2)]

            t_sig = {0: [None] * G, 1: [None] * G}
            for wi, (wbuf, t_wl) in enumerate(((wUa, t_wi), (wUb, t_wr))):
                for dg in range(G):
                    bank = pb[3 + (dg % 2)]
                    for cg in range(G):
                        deps = []
                        if cg == 0:
                            deps = t_fm_all + [t_wl, t_evac.get(id(bank))]
                        S.op("tensor", lambda q_, dg=dg, cg=cg, bank=bank, wbuf=wbuf:
                             q_.matmul(bank[:, :tc],
                                       wbuf[:, cg * C + dg * 128:cg * C + (dg + 1) * 128],
                                       bxf[:, cg * tc:(cg + 1) * tc],
                                       start=(cg == 0), stop=(cg == G - 1)),
                             deps=deps, inc=(1 if cg == G - 1 else 0))
                    t_mm = S.tok("tensor")
                    if wi == 0:
                        t_s = S.op("scalar", lambda q_, dg=dg, bank=bank: q_.activation(
                            bit[:, dg * tc:(dg + 1) * tc], bank[:, :tc],
                            AF.Sigmoid, bias=bbi[:, dg:dg + 1]),
                            deps=[t_mm, t_bi, t_ssq])
                    else:
                        t_s = S.op("scalar", lambda q_, dg=dg, bank=bank: q_.activation(
                            bat[:, dg * tc:(dg + 1) * tc], bank[:, :tc],
                            AF.Sigmoid, bias=bbr[:, dg:dg + 1]),
                            deps=[t_mm, t_br, t_sp_at_prev])
                    t_sig[wi][dg] = t_s
                    t_evac[id(bank)] = t_s
            t_at = []
            for dg in range(G):
                t_at.append(S.op("scalar", lambda q_, dg=dg: q_.activation(
                    bat[:, dg * tc:(dg + 1) * tc], bat[:, dg * tc:(dg + 1) * tc],
                    AF.Exp, scale=blam[:, 0:1]), deps=[t_sig[1][dg], t_c]))

            t_ng = S.op("vector", lambda q_: q_.scalar_tensor_tensor(
                bgx[:, :], bat[:, :], -1.0, bat[:, :], op0=ALU.mult, op1=ALU.mult),
                deps=t_at + [t_sp_gx_prev, t_bgx_r_free])
            t_cf = S.op("scalar", lambda q_: q_.activation(
                bgx[:, :], bgx[:, :], AF.Sqrt, bias=bone[:, 0:1]),
                deps=[t_ng, t_one])
            t_ix = S.op("vector", lambda q_: q_.tensor_tensor(
                bit[:, :], bit[:, :], bxf[:, :], op=ALU.mult),
                deps=[t_sig[0][dg2] for dg2 in range(G)])
            t_gx = S.op("vector", lambda q_: q_.tensor_tensor(
                bgx[:, :], bgx[:, :], bit[:, :], op=ALU.mult), deps=[t_cf, t_ix])
            t_bit_r_free = t_gx
            t_bxf_r_free = t_ix

            t_sp_at_prev = S.dma("gpsimd", "d_spat", spat_v[q], bat3, deps=t_at)
            t_sp_gx_prev = S.dma("gpsimd", "d_spgx", spgx_v[q], bgx3, deps=[t_gx])

            for g in range(G):
                initL = 0.0 if q == 0 else bcarL[:, g:g + 1]
                t_ls = S.op("vector", lambda q_, g=g, initL=initL:
                            q_.tensor_tensor_scan(
                                bscrL[:, :], bat[:, g * tc:(g + 1) * tc],
                                bgx[:, g * tc:(g + 1) * tc], initL,
                                op0=ALU.mult, op1=ALU.add),
                            deps=[t_gx, t_carL[g], t_scr_free])
                t_carL[g] = S.op("vector", lambda q_, g=g: q_.tensor_copy(
                    bcarL[:, g:g + 1], bscrL[:, tc - 1:tc]), deps=[t_ls])
                t_scr_free = t_carL[g]
                initP = 1.0 if q == 0 else bcarP[:, g:g + 1]
                t_ps = S.op("vector", lambda q_, g=g, initP=initP:
                            q_.tensor_tensor_scan(
                                bscrP[:, :], bat[:, g * tc:(g + 1) * tc],
                                bat[:, g * tc:(g + 1) * tc], initP,
                                op0=ALU.mult, op1=ALU.bypass),
                            deps=[t_carP[g], t_scr_free] + t_at)
                t_carP[g] = S.op("vector", lambda q_, g=g: q_.tensor_copy(
                    bcarP[:, g:g + 1], bscrP[:, tc - 1:tc]), deps=[t_ps])
                t_scr_free = t_carP[g]
            t_bgx_r_free = t_carP[G - 1]
            t_bx_free = t_fm_all[-1]

        t_e1 = S.dma("gpsimd", "d_ag", agin.ap()[0], bcarP[:, :],
                     deps=[t_carP[g] for g in range(G)])
        t_e2 = S.dma("gpsimd", "d_ag", agin.ap()[1], bcarL[:, :],
                     deps=[t_carL[g] for g in range(G)])
        t_cc = S.op("gpsimd", lambda q_: q_.collective_compute(
            "AllGather", ALU.bypass, replica_groups=[list(range(NCORES))],
            ins=[agin.ap().opt()], outs=[agout.ap().opt()]),
            deps=[t_e1, t_e2], sem="s_cc")
        t_bag = S.dma("sync", "d_bag",
                      bag.ap().rearrange("p (r t g) -> p r t g", t=2, g=G),
                      agout_v, deps=[t_cc])

        t_cand = [None] * NCORES
        t_cand[0] = S.op("vector", lambda q_: q_.memset(bcand[:, 0:G], 0.0))
        t_cand[4] = S.op("vector", lambda q_: q_.memset(bcand[:, 4 * G:5 * G], 0.0))
        t_btmp_r = None
        for b in range(2):
            for k in range(1, 4):
                i = 4 * b + k
                r = 4 * b + k - 1
                tm = S.op("vector", lambda q_, i=i, r=r: q_.tensor_tensor(
                    btmp8[:, :], bag[:, r * 16:r * 16 + G],
                    bcand[:, (i - 1) * G:i * G], op=ALU.mult),
                    deps=[t_bag, t_cand[i - 1], t_btmp_r])
                t_cand[i] = S.op("vector", lambda q_, i=i, r=r: q_.tensor_tensor(
                    bcand[:, i * G:(i + 1) * G], bag[:, r * 16 + G:r * 16 + 2 * G],
                    btmp8[:, :], op=ALU.add), deps=[tm])
                t_btmp_r = t_cand[i]
        t_h = S.op("vector", lambda q_: q_.tensor_scalar(
            bhp[:, 0:G], bcand[:, 0:G], bcsel[:, 0:1], None, op0=ALU.mult),
            deps=[t_cand[0], t_cs])
        for i in range(1, NCORES):
            src = ((i + 1) % 2) * G
            dst = (i % 2) * G
            t_h = S.op("vector", lambda q_, i=i, src=src, dst=dst:
                       q_.scalar_tensor_tensor(
                           bhp[:, dst:dst + G], bcand[:, i * G:(i + 1) * G],
                           bcsel[:, i:i + 1], bhp[:, src:src + G],
                           op0=ALU.mult, op1=ALU.add),
                       deps=[t_cand[i], t_h])
        t_hrun = S.op("vector", lambda q_: q_.tensor_copy(
            bHrun[:, :], bhp[:, G:2 * G]), deps=[t_h])

        t_pa_pe = S.tok("tensor")
        t_lat_free = t_sp_at_prev
        t_lgx_free = t_sp_gx_prev
        t_bx_free_b = t_bx_free
        t_bit_free = t_bit_r_free
        t_bxf_war = [t_bxf_r_free]
        t_hup = [t_hrun] * G
        t_out_prev = None
        t_up_used = {0: t_pa_pe, 1: t_pa_pe}
        t_dn_used = None
        t_bht_war = None
        t_hid_prev = None
        sp_at_all = ("d_spat", 16 * CH)
        sp_gx_all = ("d_spgx", 16 * CH)

        for q in range(CH):
            t_lat = S.dma("sync", "d_lat", bat3, spat_v[q],
                          deps=[sp_at_all, t_lat_free])
            t_lgx = S.dma("sync", "d_lgx", bgx3, spgx_v[q],
                          deps=[sp_gx_all, t_lgx_free])
            t_lx = S.dma("sync", "d_x", bx3, x_v[q], deps=[t_bx_free_b])

            t_scan = [None] * G
            for g in range(G):
                t_scan[g] = S.op("vector", lambda q_, g=g: q_.tensor_tensor_scan(
                    bit[:, g * tc:(g + 1) * tc], bat[:, g * tc:(g + 1) * tc],
                    bgx[:, g * tc:(g + 1) * tc], bHrun[:, g:g + 1],
                    op0=ALU.mult, op1=ALU.add),
                    deps=[t_lat, t_lgx, t_hup[g], t_bit_free])
                t_hup[g] = S.op("vector", lambda q_, g=g: q_.tensor_copy(
                    bHrun[:, g:g + 1], bit[:, g * tc + tc - 1:g * tc + tc]),
                    deps=[t_scan[g]])
            t_lat_free = t_scan[G - 1]
            t_lgx_free = t_scan[G - 1]

            t_res = []
            for u in range(NT):
                for hb in range(2):
                    bank = pb[hb]
                    for k in range(4):
                        g = hb * 4 + k
                        S.op("tensor", lambda q_, u=u, g=g, k=k, bank=bank:
                             q_.transpose(bank[:, k * 128:(k + 1) * 128],
                                          bit[:, g * tc + u * 128:g * tc + (u + 1) * 128],
                                          bident[:, :]),
                             deps=([t_scan[hb * 4 + kk] for kk in range(4)]
                                   + [t_id, t_evac.get(id(bank))]
                                   if k == 0 else []),
                             inc=(1 if k == 3 else 0))
                    t_tp = S.tok("tensor")
                    t_r = S.op("vector", lambda q_, u=u, hb=hb, bank=bank:
                               q_.tensor_tensor(
                                   bacc[:, u * C + hb * 512:u * C + (hb + 1) * 512],
                                   bank[:, :512],
                                   bx[:, u * C + hb * 512:u * C + (hb + 1) * 512],
                                   op=ALU.add),
                               deps=[t_tp, t_lx, t_out_prev])
                    t_evac[id(bank)] = t_r
                    t_res.append(t_r)
            t_bx_free_b = t_res[-1]

            t_norm, _ = ln_block(
                bacc.ap(), bacc3, bht.ap()[:, :NT * C], bht_uc, bit.ap(),
                deps_in=t_res + [t_bht_war],
                extra_norm_deps=[t_hup[G - 1]])

            t_x2f = tmfm_transpose(bit.ap(), bxf3, t_norm, t_bxf_war)
            t_x2f_all = [t_x2f[u][hb] for u in range(NT) for hb in range(2)]

            t_gate = [None] * NT
            prbank = pb[2]
            for u in range(NT):
                for cg in range(G):
                    deps = []
                    if cg == 0:
                        deps = [t_x2f[u][0], t_x2f[u][1], t_rw,
                                t_evac.get(id(prbank))]
                    S.op("tensor", lambda q_, u=u, cg=cg: q_.matmul(
                        prbank[:, :E],
                        bxf[:, cg * tc + u * 128:cg * tc + (u + 1) * 128],
                        brw[:, cg * E:(cg + 1) * E],
                        start=(cg == 0), stop=(cg == G - 1)),
                        deps=deps, inc=(1 if cg == G - 1 else 0))
                t_rmm = S.tok("tensor")
                t_m1 = S.op("vector", lambda q_, u=u: q_.reduce_max(
                    bnm1[:, u:u + 1], prbank[:, :E], axis=AX.X, negate=True),
                    deps=[t_rmm])
                t_ex = S.op("scalar", lambda q_, u=u: q_.activation(
                    bge[:, u * E:(u + 1) * E], prbank[:, :E], AF.Exp,
                    bias=bnm1[:, u:u + 1]), deps=[t_m1])
                t_mk = S.op("vector", lambda q_, u=u: q_.tensor_scalar(
                    bmsk[:, u * E:(u + 1) * E], prbank[:, :E],
                    bnm1[:, u:u + 1], 0.0, op0=ALU.add, op1=ALU.is_ge),
                    deps=[t_m1])
                t_mk2 = S.op("vector", lambda q_, u=u: q_.scalar_tensor_tensor(
                    bmsk[:, u * E:(u + 1) * E], bmsk[:, u * E:(u + 1) * E],
                    -BIG, prbank[:, :E], op0=ALU.mult, op1=ALU.add),
                    deps=[t_mk])
                t_m2r = S.op("vector", lambda q_, u=u: q_.reduce_max(
                    bnm2[:, u:u + 1], bmsk[:, u * E:(u + 1) * E], axis=AX.X,
                    negate=True), deps=[t_mk2])
                t_se = S.op("vector", lambda q_, u=u: q_.tensor_scalar(
                    bsel[:, u * E:(u + 1) * E], prbank[:, :E],
                    bnm2[:, u:u + 1], 0.0, op0=ALU.add, op1=ALU.is_ge),
                    deps=[t_m2r])
                t_g0 = S.op("vector", lambda q_, u=u: q_.tensor_tensor(
                    bge[:, u * E:(u + 1) * E], bge[:, u * E:(u + 1) * E],
                    bsel[:, u * E:(u + 1) * E], op=ALU.mult), deps=[t_se, t_ex])
                t_evac[id(prbank)] = t_g0
                t_gs = S.op("vector", lambda q_, u=u: q_.reduce_sum(
                    bgs[:, u:u + 1], bge[:, u * E:(u + 1) * E], axis=AX.X),
                    deps=[t_g0])
                t_gr = S.op("vector", lambda q_, u=u: q_.reciprocal(
                    bgr[:, u:u + 1], bgs[:, u:u + 1]), deps=[t_gs])
                t_gate[u] = S.op("vector", lambda q_, u=u: q_.tensor_scalar(
                    bgate[:, u * E:(u + 1) * E], bge[:, u * E:(u + 1) * E],
                    bgr[:, u:u + 1], None, op0=ALU.mult), deps=[t_gr])

            t_acc_last = t_res
            for e in range(E):
                upb = {0: wUa, 1: wUb}
                t_up = {}
                for j in range(2):
                    S.dma("sync", "d_up%d" % j,
                          upb[j].ap().rearrange("p (g s d2) -> p g s d2",
                                                s=2, d2=512)[:, :, 0, :],
                          up_v[e][:, :, j, :], deps=[t_up_used[j]])
                    t_up[j] = S.dma("sync", "d_up%d" % j,
                                    upb[j].ap().rearrange("p (g s d2) -> p g s d2",
                                                          s=2, d2=512)[:, :, 1, :],
                                    up_v[e][:, :, 2 + j, :], deps=[t_up_used[j]])
                t_htc = [[None] * 2 for _ in range(NT)]
                for j in range(2):
                    for u in range(NT):
                        for which, bank in ((0, pb[3]), (1, pb[4])):
                            for cg in range(G):
                                deps = []
                                if cg == 0:
                                    deps = [t_up[j], t_evac.get(id(bank)),
                                            t_x2f[u][0], t_x2f[u][1]]
                                S.op("tensor", lambda q_, u=u, cg=cg, j=j,
                                     which=which, bank=bank, wb=upb[j]:
                                     q_.matmul(
                                         bank[:, :512],
                                         bxf[:, cg * tc + u * 128:cg * tc + (u + 1) * 128],
                                         wb[:, cg * C + which * 512:cg * C + (which + 1) * 512],
                                         start=(cg == 0), stop=(cg == G - 1)),
                                     deps=deps, inc=(1 if cg == G - 1 else 0))
                            if which == 0:
                                t_h1 = S.tok("tensor")
                            else:
                                t_h2 = S.tok("tensor")
                        t_si = S.op("scalar", lambda q_: q_.activation(
                            bsilu[:, :], pb[3][:, :512], AF.Silu),
                            deps=[t_h1, t_hid_prev])
                        t_evac[id(pb[3])] = t_si
                        t_hid = S.op("vector", lambda q_, u=u, e=e:
                                     q_.scalar_tensor_tensor(
                                         bhid[:, :], pb[4][:, :512],
                                         bgate[:, u * E + e:u * E + e + 1],
                                         bsilu[:, :], op0=ALU.mult, op1=ALU.mult),
                                     deps=[t_h2, t_si, t_gate[u]])
                        t_evac[id(pb[4])] = t_hid
                        t_hid_prev = t_hid
                        for k in range(4):
                            S.op("tensor", lambda q_, k=k: q_.transpose(
                                pb[5][:, k * 128:(k + 1) * 128],
                                bhid[:, k * 128:(k + 1) * 128], bident[:, :]),
                                deps=([t_hid, t_evac.get(id(pb[5])), t_id]
                                      if k == 0 else []),
                                inc=(1 if k == 3 else 0))
                        t_ht = S.tok("tensor")
                        t_htc[u][j] = S.op("vector", lambda q_, u=u, j=j:
                                           q_.tensor_copy(
                                               bht[:, u * C + j * 512:u * C + (j + 1) * 512],
                                               pb[5][:, :512]),
                                           deps=[t_ht])
                        t_evac[id(pb[5])] = t_htc[u][j]

                for r in range(2):
                    t_dn = S.dma("sync", "d_dn",
                                 wD.ap().rearrange("p (g h) -> p g h", h=C),
                                 dn_v[e][:, r * 4:(r + 1) * 4, :],
                                 deps=[t_dn_used])
                    t_acc_new = []
                    for u in range(NT):
                        for hc in range(2):
                            bank = pb[6 + hc]
                            for dgl in range(4):
                                dg = r * 4 + dgl
                                deps = []
                                if dgl == 0:
                                    deps = [t_dn, t_htc[u][0], t_htc[u][1],
                                            t_evac.get(id(bank))]
                                S.op("tensor", lambda q_, u=u, hc=hc, dg=dg,
                                     dgl=dgl, bank=bank:
                                     q_.matmul(
                                         bank[:, :512],
                                         bht[:, u * C + dg * 128:u * C + (dg + 1) * 128],
                                         wD[:, dgl * C + hc * 512:dgl * C + (hc + 1) * 512],
                                         start=(dgl == 0), stop=(dgl == 3)),
                                     deps=deps, inc=(1 if dgl == 3 else 0))
                            t_dmm = S.tok("tensor")
                            t_a = S.op("vector", lambda q_, u=u, hc=hc, bank=bank:
                                       q_.tensor_tensor(
                                           bacc[:, u * C + hc * 512:u * C + (hc + 1) * 512],
                                           bacc[:, u * C + hc * 512:u * C + (hc + 1) * 512],
                                           bank[:, :512], op=ALU.add),
                                       deps=[t_dmm, t_acc_last[u * 2 + hc]])
                            t_evac[id(bank)] = t_a
                            t_acc_new.append(t_a)
                    t_acc_last = t_acc_new
                    t_dn_used = S.tok("tensor")
                t_pe_now = S.tok("tensor")
                t_up_used = {0: t_pe_now, 1: t_pe_now}
                t_dn_used = t_pe_now
                t_bht_war = t_pe_now
                t_bxf_war = [t_pe_now]

            t_out_prev = S.dma("gpsimd", "d_out", out_v[q], bacc3,
                               deps=t_acc_last)
            t_bit_free = t_x2f_all[-1]

        S.fence("sync", [("d_out", 16 * CH), t_out_prev])
        S.build(block)
    return nc


def _stage(inputs, tok):
    x = np.ascontiguousarray(np.asarray(inputs["L0_attn_x"], np.float32)
                             .reshape(-1, C))
    wiT = np.ascontiguousarray(np.asarray(inputs["L0_attn_Wi"], np.float32).T)
    wrT = np.ascontiguousarray(np.asarray(inputs["L0_attn_Wr"], np.float32).T)
    bi = np.ascontiguousarray(np.asarray(inputs["L0_attn_bi"], np.float32)
                              .reshape(G, 128).T)
    br = np.ascontiguousarray(np.asarray(inputs["L0_attn_br"], np.float32)
                              .reshape(G, 128).T)
    lam = np.ascontiguousarray(
        np.tile(np.asarray(inputs["L0_attn_cap_lambda"], np.float32)
                .reshape(1, 1), (128, 1)))
    rw = np.ascontiguousarray(np.asarray(inputs["L0_ffn_router_w"], np.float32).T)
    up = np.ascontiguousarray(np.asarray(inputs["L0_ffn_up_proj"], np.float32))
    dn = np.ascontiguousarray(np.asarray(inputs["L0_ffn_down_proj"], np.float32))
    in_maps = []
    for c in range(NCORES):
        csel = np.ascontiguousarray(
            np.tile(np.eye(NCORES, dtype=np.float32)[c], (128, 1)))
        in_maps.append({
            "x": np.ascontiguousarray(x[c * tok:(c + 1) * tok]),
            "wiT": wiT, "wrT": wrT, "bi": bi, "br": br, "lam": lam,
            "rw": rw, "up": up, "dn": dn, "csel": csel,
        })
    return in_maps


def kernel(**inputs):
    B, T, _ = inputs["L0_attn_x"].shape
    tok = B * T // NCORES
    in_maps = _stage(inputs, tok)
    nc = build_program(tok=tok, tc=512)
    res = run_bass_kernel_spmd(nc, in_maps, core_ids=list(range(NCORES)))
    out = np.concatenate([np.asarray(r["out"]) for r in res.results], axis=0)
    return out.reshape(B, T, C).astype(np.float32)


# version 02
# speedup vs baseline: 1.0704x; 1.0704x over previous
import contextlib
import numpy as np

import concourse.bass as bass
import concourse.mybir as mybir
from concourse.bass_utils import run_bass_kernel_spmd

F32 = mybir.dt.float32
AF = mybir.ActivationFunctionType
ALU = mybir.AluOpType
AX = mybir.AxisListType

NCORES = 8
C = 1024
G = 8
E = 8
DFF = 2048
EPS = 1e-5
BIG = 1e9


class Sched:

    ENGS = ("sync", "tensor", "vector", "scalar", "gpsimd")

    def __init__(self, nc, es):
        self.nc = nc
        self.es = es
        self.items = {e: [] for e in self.ENGS}
        self.cnt = {}
        self.sems = {}

    def _sem(self, name):
        if name not in self.sems:
            self.sems[name] = self.es.enter_context(self.nc.semaphore(name))
            self.cnt[name] = 0
        return self.sems[name]

    def op(self, eng, fn, deps=(), inc=1, sem=None):
        if sem is None:
            sem = "e_" + eng
        self._sem(sem)
        waits = {}
        for d in deps:
            if d is None:
                continue
            s, v = d
            self._sem(s)
            if waits.get(s, 0) < v:
                waits[s] = v
        if inc:
            self.cnt[sem] += inc
        self.items[eng].append((fn, waits, sem if inc else None, inc))
        return (sem, self.cnt[sem]) if inc else None

    def tok(self, eng):
        sem = "e_" + eng
        self._sem(sem)
        return (sem, self.cnt[sem])

    def dma(self, eng, sem, out, in_, deps=()):
        def fn(q, out=out, in_=in_):
            return q.dma_start(out=out, in_=in_)
        self._sem(sem)
        deps = list(deps) + [(sem, self.cnt[sem])]
        return self.op(eng, fn, deps=deps, inc=16, sem=sem)

    def fence(self, eng, deps):
        return self.op(eng, lambda q: None, deps=deps, inc=0)

    def build(self, block):
        for eng in self.ENGS:
            items = self.items[eng]
            if not items:
                continue

            def body(q, items=items):
                vclock = {}
                for fn, waits, sem, inc in items:
                    for s in sorted(waits):
                        v = waits[s]
                        if vclock.get(s, 0) < v:
                            q.wait_ge(self.sems[s], v)
                            vclock[s] = v
                    inst = fn(q)
                    if inst is not None and sem is not None:
                        inst.then_inc(self.sems[sem], inc)

            getattr(block, eng)(body)


def build_program(tok=4096, tc=512):
    CH = tok // tc
    NT = tc // 128
    assert NT == 4 and tok % tc == 0

    nc = bass.Bass("TRN2", target_bir_lowering=False, debug=False)

    x_d = nc.dram_tensor("x", [tok, C], F32, kind="ExternalInput").ap()
    wiT_d = nc.dram_tensor("wiT", [C, C], F32, kind="ExternalInput").ap()
    wrT_d = nc.dram_tensor("wrT", [C, C], F32, kind="ExternalInput").ap()
    bi_d = nc.dram_tensor("bi", [128, G], F32, kind="ExternalInput").ap()
    br_d = nc.dram_tensor("br", [128, G], F32, kind="ExternalInput").ap()
    lam_d = nc.dram_tensor("lam", [128, 1], F32, kind="ExternalInput").ap()
    rw_d = nc.dram_tensor("rw", [C, E], F32, kind="ExternalInput").ap()
    up_d = nc.dram_tensor("up", [E, C, DFF], F32, kind="ExternalInput").ap()
    dn_d = nc.dram_tensor("dn", [E, C, C], F32, kind="ExternalInput").ap()
    csel_d = nc.dram_tensor("csel", [128, NCORES], F32, kind="ExternalInput").ap()
    out_d = nc.dram_tensor("out", [tok, C], F32, kind="ExternalOutput").ap()

    sp_at = nc.dram_tensor("sp_at", [CH, G, 128, tc], F32, kind="Internal").ap()
    sp_gx = nc.dram_tensor("sp_gx", [CH, G, 128, tc], F32, kind="Internal").ap()
    agin = nc.dram_tensor("agin", [2, 128, G], F32, kind="Internal")
    agout = nc.dram_tensor("agout", [NCORES, 2, 128, G], F32, kind="Internal")

    x_v = x_d.rearrange("(q u p) c -> q p u c", p=128, u=NT)
    out_v = out_d.rearrange("(q u p) c -> q p u c", p=128, u=NT)
    spat_v = sp_at.rearrange("q g p t -> q p g t")
    spgx_v = sp_gx.rearrange("q g p t -> q p g t")
    wiT_v = wiT_d.rearrange("(cg p) d -> p cg d", p=128)
    wrT_v = wrT_d.rearrange("(cg p) d -> p cg d", p=128)
    rw_v = rw_d.rearrange("(cg p) e -> p cg e", p=128)
    up_v = up_d.rearrange("e (cg p) (hh d1) -> e hh p cg d1", p=128, hh=2)
    dn_v = dn_d.rearrange("e (dg p) h -> e p dg h", p=128)
    agout_v = agout.ap().rearrange("r t p g -> p r t g")

    es = contextlib.ExitStack()
    with es:
        bx = nc.alloc_sbuf_tensor("bx", [128, NT * C], F32)
        bxf = nc.alloc_sbuf_tensor("bxf", [128, G * tc], F32)
        bit = nc.alloc_sbuf_tensor("bit", [128, G * tc], F32)
        bat = nc.alloc_sbuf_tensor("bat", [128, G * tc], F32)
        bgx = nc.alloc_sbuf_tensor("bgx", [128, G * tc], F32)
        bacc = nc.alloc_sbuf_tensor("bacc", [128, NT * C], F32)
        bht = nc.alloc_sbuf_tensor("bht", [128, NT * C], F32)
        wUa = nc.alloc_sbuf_tensor("wUa", [128, G * C], F32)
        wUb = nc.alloc_sbuf_tensor("wUb", [128, G * C], F32)
        wD = nc.alloc_sbuf_tensor("wD", [128, 4 * C], F32)
        bident = nc.alloc_sbuf_tensor("bident", [128, 128], F32)
        bbi = nc.alloc_sbuf_tensor("bbi", [128, G], F32)
        bbr = nc.alloc_sbuf_tensor("bbr", [128, G], F32)
        brw = nc.alloc_sbuf_tensor("brw", [128, G * E], F32)
        blam = nc.alloc_sbuf_tensor("blam", [128, 1], F32)
        beps = nc.alloc_sbuf_tensor("beps", [128, 1], F32)
        bone = nc.alloc_sbuf_tensor("bone", [128, 1], F32)
        bcsel = nc.alloc_sbuf_tensor("bcsel", [128, NCORES], F32)
        bscrL = nc.alloc_sbuf_tensor("bscrL", [128, tc], F32)
        bscrP = bscrL
        bsilu = nc.alloc_sbuf_tensor("bsilu", [128, 512], F32)
        bcarL = nc.alloc_sbuf_tensor("bcarL", [128, G], F32)
        bcarP = nc.alloc_sbuf_tensor("bcarP", [128, G], F32)
        bag = nc.alloc_sbuf_tensor("bag", [128, NCORES * 2 * G], F32)
        bcand = nc.alloc_sbuf_tensor("bcand", [128, NCORES * G], F32)
        bhp = nc.alloc_sbuf_tensor("bhp", [128, 2 * G], F32)
        bHrun = nc.alloc_sbuf_tensor("bHrun", [128, G], F32)
        btmp8 = nc.alloc_sbuf_tensor("btmp8", [128, G], F32)
        bsum = nc.alloc_sbuf_tensor("bsum", [128, NT], F32)
        bssq = nc.alloc_sbuf_tensor("bssq", [128, NT], F32)
        bnm = nc.alloc_sbuf_tensor("bnm", [128, NT], F32)
        bm2t = nc.alloc_sbuf_tensor("bm2t", [128, NT], F32)
        bvar = nc.alloc_sbuf_tensor("bvar", [128, NT], F32)
        bstd = nc.alloc_sbuf_tensor("bstd", [128, NT], F32)
        brstd = nc.alloc_sbuf_tensor("brstd", [128, NT], F32)
        bnm1 = nc.alloc_sbuf_tensor("bnm1", [128, NT], F32)
        bnm2 = nc.alloc_sbuf_tensor("bnm2", [128, NT], F32)
        bgs = nc.alloc_sbuf_tensor("bgs", [128, NT], F32)
        bgr = nc.alloc_sbuf_tensor("bgr", [128, NT], F32)
        bge = nc.alloc_sbuf_tensor("bge", [128, NT * E], F32)
        bmsk = nc.alloc_sbuf_tensor("bmsk", [128, NT * E], F32)
        bsel = nc.alloc_sbuf_tensor("bsel", [128, NT * E], F32)
        bgate = nc.alloc_sbuf_tensor("bgate", [128, NT * E], F32)

        pb = [nc.alloc_psum_tensor("pb%d" % i, [128, 512], F32) for i in range(8)]

        S = Sched(nc, es)
        block = es.enter_context(nc.Block())

        bx3 = bx.ap().rearrange("p (u c) -> p u c", c=C)
        bacc3 = bacc.ap().rearrange("p (u c) -> p u c", c=C)
        bat3 = bat.ap().rearrange("p (g t) -> p g t", t=tc)
        bgx3 = bgx.ap().rearrange("p (g t) -> p g t", t=tc)
        bxf3 = bxf.ap().rearrange("p (g t) -> p g t", t=tc)
        bit_uc = bit.ap().rearrange("p (u c) -> p u c", c=C)
        bht_uc = bht.ap().rearrange("p (u c) -> p u c", c=C)

        t_wi = S.dma("sync", "d_w", wUa.ap().rearrange("p (g c) -> p g c", c=C), wiT_v)
        t_wr = S.dma("sync", "d_w", wUb.ap().rearrange("p (g c) -> p g c", c=C), wrT_v)
        t_bi = S.dma("sync", "d_w", bbi[:, :], bi_d)
        t_br = S.dma("sync", "d_w", bbr[:, :], br_d)
        t_rw = S.dma("sync", "d_w", brw.ap().rearrange("p (g e) -> p g e", e=E), rw_v)
        t_lam = S.dma("sync", "d_w", blam[:, :], lam_d)
        t_cs = S.dma("sync", "d_w", bcsel[:, :], csel_d)

        t_id0 = S.op("gpsimd", lambda q: q.memset(bident[:, :], 1.0))
        t_eps = S.op("vector", lambda q: q.memset(beps[:, :], EPS))
        t_one = S.op("vector", lambda q: q.memset(bone[:, :], 1.0))
        t_id = S.op("gpsimd", lambda q: q.affine_select(
            bident[:, :], bident[:, :], pattern=[[1, 128]],
            compare_op=ALU.is_equal, fill=0.0, channel_multiplier=-1),
            deps=[t_id0])

        t_c1 = S.op("scalar", lambda q: q.activation(blam[:, :], blam[:, :], AF.Exp),
                    deps=[t_lam])
        t_c2 = S.op("scalar", lambda q: q.activation(blam[:, :], blam[:, :], AF.Ln,
                                                     bias=bone[:, 0:1]),
                    deps=[t_c1, t_one])
        t_c = S.op("scalar", lambda q: q.mul(blam[:, :], blam[:, :], -8.0), deps=[t_c2])

        def ln_block(src_ap, src3, sq_dst, sq3, norm_dst, deps_in, extra_norm_deps):
            t_sq = S.op("scalar", lambda q, a=sq_dst, b=src_ap:
                        q.activation(a, b, AF.Square), deps=deps_in)
            t_sums = S.op("vector", lambda q, i=src3: q.reduce_sum(
                bsum[:, :], i, axis=AX.X), deps=deps_in)
            t_ssq = S.op("vector", lambda q, i=sq3: q.reduce_sum(
                bssq[:, :], i, axis=AX.X), deps=[t_sq])
            t_nm = S.op("vector", lambda q: q.tensor_scalar(
                bnm[:, :], bsum[:, :], -1.0 / C, None, op0=ALU.mult), deps=[t_sums])
            t_m2 = S.op("vector", lambda q: q.tensor_tensor(
                bm2t[:, :], bnm[:, :], bnm[:, :], op=ALU.mult), deps=[t_nm])
            t_var = S.op("vector", lambda q: q.scalar_tensor_tensor(
                bvar[:, :], bssq[:, :], 1.0 / C, bm2t[:, :],
                op0=ALU.mult, op1=ALU.subtract), deps=[t_ssq, t_m2])
            t_std = S.op("scalar", lambda q: q.activation(
                bstd[:, :], bvar[:, :], AF.Sqrt, bias=beps[:, 0:1]),
                deps=[t_var, t_eps])
            t_rstd = S.op("vector", lambda q: q.reciprocal(
                brstd[:, :], bstd[:, :]), deps=[t_std])
            t_norm = []
            for u in range(NT):
                t_norm.append(S.op("vector", lambda q, u=u, s=src_ap.tensor,
                                   d=norm_dst: q.tensor_scalar(
                                       d[:, u * C:(u + 1) * C],
                                       s[:, u * C:(u + 1) * C],
                                       bnm[:, u:u + 1], brstd[:, u:u + 1],
                                       op0=ALU.add, op1=ALU.mult),
                                   deps=[t_rstd] + list(extra_norm_deps)))
            return t_norm, t_ssq

        def tmfm_transpose(src, dst3, t_src_ready, dst_war):
            t_cp = [[None] * 2 for _ in range(NT)]
            for u in range(NT):
                for hb in range(2):
                    bank = pb[hb]
                    for k in range(4):
                        g = hb * 4 + k
                        S.op("tensor", lambda q, u=u, g=g, k=k, bank=bank, s=src:
                             q.transpose(bank[:, k * 128:(k + 1) * 128],
                                         s[:, u * C + g * 128:u * C + (g + 1) * 128],
                                         bident[:, :]),
                             deps=([t_src_ready[u], t_id, t_evac.get(id(bank))]
                                   if k == 0 else []),
                             inc=(1 if k == 3 else 0))
                    t_tp = S.tok("tensor")
                    d = dst3[:, hb * 4:hb * 4 + 4, u * 128:(u + 1) * 128]
                    c = S.op("vector", lambda q, d=d, bank=bank: q.tensor_copy(
                        d, bank.ap().rearrange("p (k t) -> p k t", t=128)),
                        deps=[t_tp] + list(dst_war))
                    t_evac[id(bank)] = c
                    t_cp[u][hb] = c
            return t_cp

        t_evac = {}
        t_bx_free = None
        t_bit_r_free = None
        t_bxf_r_free = None
        t_bgx_r_free = None
        t_bacc_scr = None
        t_sp_at_prev = None
        t_sp_gx_prev = None
        t_carL = [None] * G
        t_carP = [None] * G
        t_scr_free = None
        bacc_uc = bacc.ap().rearrange("p (u c) -> p u c", c=C)

        for q in range(CH):
            t_lx = S.dma("sync", "d_x", bx3, x_v[q], deps=[t_bx_free])
            t_norm, t_ssq = ln_block(
                bx.ap(), bx3, bacc.ap()[:, :NT * C], bacc_uc, bx.ap(),
                deps_in=[t_lx, t_bacc_scr], extra_norm_deps=[])
            t_bacc_scr = t_ssq
            t_fmcp = tmfm_transpose(bx.ap(), bxf3, t_norm, [t_bxf_r_free])
            t_fm_all = [t_fmcp[u][hb] for u in range(NT) for hb in range(2)]

            t_sig = {0: [None] * G, 1: [None] * G}
            for wi, (wbuf, t_wl) in enumerate(((wUa, t_wi), (wUb, t_wr))):
                for dg in range(G):
                    bank = pb[3 + (dg % 2)]
                    for cg in range(G):
                        deps = []
                        if cg == 0:
                            deps = t_fm_all + [t_wl, t_evac.get(id(bank))]
                        S.op("tensor", lambda q_, dg=dg, cg=cg, bank=bank, wbuf=wbuf:
                             q_.matmul(bank[:, :tc],
                                       wbuf[:, cg * C + dg * 128:cg * C + (dg + 1) * 128],
                                       bxf[:, cg * tc:(cg + 1) * tc],
                                       start=(cg == 0), stop=(cg == G - 1)),
                             deps=deps, inc=(1 if cg == G - 1 else 0))
                    t_mm = S.tok("tensor")
                    if wi == 0:
                        t_s = S.op("scalar", lambda q_, dg=dg, bank=bank: q_.activation(
                            bit[:, dg * tc:(dg + 1) * tc], bank[:, :tc],
                            AF.Sigmoid, bias=bbi[:, dg:dg + 1]),
                            deps=[t_mm, t_bi, t_bit_r_free])
                    else:
                        t_s = S.op("scalar", lambda q_, dg=dg, bank=bank: q_.activation(
                            bat[:, dg * tc:(dg + 1) * tc], bank[:, :tc],
                            AF.Sigmoid, bias=bbr[:, dg:dg + 1]),
                            deps=[t_mm, t_br, t_sp_at_prev])
                    t_sig[wi][dg] = t_s
                    t_evac[id(bank)] = t_s
            t_at = []
            for dg in range(G):
                t_at.append(S.op("scalar", lambda q_, dg=dg: q_.activation(
                    bat[:, dg * tc:(dg + 1) * tc], bat[:, dg * tc:(dg + 1) * tc],
                    AF.Exp, scale=blam[:, 0:1]), deps=[t_sig[1][dg], t_c]))

            t_ixg = [None] * G
            t_ngg = [None] * G
            for dg in range(G):
                t_ixg[dg] = S.op("vector", lambda q_, dg=dg: q_.tensor_tensor(
                    bit[:, dg * tc:(dg + 1) * tc], bit[:, dg * tc:(dg + 1) * tc],
                    bxf[:, dg * tc:(dg + 1) * tc], op=ALU.mult),
                    deps=[t_sig[0][dg]])
                t_ngg[dg] = S.op("vector", lambda q_, dg=dg: q_.scalar_tensor_tensor(
                    bgx[:, dg * tc:(dg + 1) * tc], bat[:, dg * tc:(dg + 1) * tc],
                    -1.0, bat[:, dg * tc:(dg + 1) * tc],
                    op0=ALU.mult, op1=ALU.mult),
                    deps=[t_at[dg], t_sp_gx_prev, t_bgx_r_free])
            t_cfg = [None] * G
            for dg in range(G):
                t_cfg[dg] = S.op("scalar", lambda q_, dg=dg: q_.activation(
                    bgx[:, dg * tc:(dg + 1) * tc], bgx[:, dg * tc:(dg + 1) * tc],
                    AF.Sqrt, bias=bone[:, 0:1]), deps=[t_ngg[dg], t_one])
            t_gxg = [None] * G
            for g in range(G):
                t_gxg[g] = S.op("vector", lambda q_, g=g: q_.tensor_tensor(
                    bgx[:, g * tc:(g + 1) * tc], bgx[:, g * tc:(g + 1) * tc],
                    bit[:, g * tc:(g + 1) * tc], op=ALU.mult),
                    deps=[t_cfg[g], t_ixg[g]])
                initL = 0.0 if q == 0 else bcarL[:, g:g + 1]
                t_ls = S.op("vector", lambda q_, g=g, initL=initL:
                            q_.tensor_tensor_scan(
                                bscrL[:, :], bat[:, g * tc:(g + 1) * tc],
                                bgx[:, g * tc:(g + 1) * tc], initL,
                                op0=ALU.mult, op1=ALU.add),
                            deps=[t_gxg[g], t_carL[g], t_scr_free])
                t_carL[g] = S.op("vector", lambda q_, g=g: q_.tensor_copy(
                    bcarL[:, g:g + 1], bscrL[:, tc - 1:tc]), deps=[t_ls])
                t_scr_free = t_carL[g]
                initP = 1.0 if q == 0 else bcarP[:, g:g + 1]
                t_ps = S.op("vector", lambda q_, g=g, initP=initP:
                            q_.tensor_tensor_scan(
                                bscrP[:, :], bat[:, g * tc:(g + 1) * tc],
                                bat[:, g * tc:(g + 1) * tc], initP,
                                op0=ALU.mult, op1=ALU.bypass),
                            deps=[t_carP[g], t_scr_free, t_at[g]])
                t_carP[g] = S.op("vector", lambda q_, g=g: q_.tensor_copy(
                    bcarP[:, g:g + 1], bscrP[:, tc - 1:tc]), deps=[t_ps])
                t_scr_free = t_carP[g]

            t_sp_at_prev = S.dma("gpsimd", "d_spat", spat_v[q], bat3, deps=t_at)
            t_sp_gx_prev = S.dma("gpsimd", "d_spgx", spgx_v[q], bgx3,
                                 deps=list(t_gxg))
            t_bit_r_free = t_gxg[G - 1]
            t_bxf_r_free = t_ixg[G - 1]
            t_bgx_r_free = t_carP[G - 1]
            t_bx_free = t_fm_all[-1]

        t_e1 = S.dma("gpsimd", "d_ag", agin.ap()[0], bcarP[:, :],
                     deps=[t_carP[g] for g in range(G)])
        t_e2 = S.dma("gpsimd", "d_ag", agin.ap()[1], bcarL[:, :],
                     deps=[t_carL[g] for g in range(G)])
        t_cc = S.op("gpsimd", lambda q_: q_.collective_compute(
            "AllGather", ALU.bypass, replica_groups=[list(range(NCORES))],
            ins=[agin.ap().opt()], outs=[agout.ap().opt()]),
            deps=[t_e1, t_e2], sem="s_cc")
        t_bag = S.dma("sync", "d_bag",
                      bag.ap().rearrange("p (r t g) -> p r t g", t=2, g=G),
                      agout_v, deps=[t_cc])

        t_cand = [None] * NCORES
        t_cand[0] = S.op("vector", lambda q_: q_.memset(bcand[:, 0:G], 0.0))
        t_cand[4] = S.op("vector", lambda q_: q_.memset(bcand[:, 4 * G:5 * G], 0.0))
        t_btmp_r = None
        for b in range(2):
            for k in range(1, 4):
                i = 4 * b + k
                r = 4 * b + k - 1
                tm = S.op("vector", lambda q_, i=i, r=r: q_.tensor_tensor(
                    btmp8[:, :], bag[:, r * 16:r * 16 + G],
                    bcand[:, (i - 1) * G:i * G], op=ALU.mult),
                    deps=[t_bag, t_cand[i - 1], t_btmp_r])
                t_cand[i] = S.op("vector", lambda q_, i=i, r=r: q_.tensor_tensor(
                    bcand[:, i * G:(i + 1) * G], bag[:, r * 16 + G:r * 16 + 2 * G],
                    btmp8[:, :], op=ALU.add), deps=[tm])
                t_btmp_r = t_cand[i]
        t_h = S.op("vector", lambda q_: q_.tensor_scalar(
            bhp[:, 0:G], bcand[:, 0:G], bcsel[:, 0:1], None, op0=ALU.mult),
            deps=[t_cand[0], t_cs])
        for i in range(1, NCORES):
            src = ((i + 1) % 2) * G
            dst = (i % 2) * G
            t_h = S.op("vector", lambda q_, i=i, src=src, dst=dst:
                       q_.scalar_tensor_tensor(
                           bhp[:, dst:dst + G], bcand[:, i * G:(i + 1) * G],
                           bcsel[:, i:i + 1], bhp[:, src:src + G],
                           op0=ALU.mult, op1=ALU.add),
                       deps=[t_cand[i], t_h])
        t_hrun = S.op("vector", lambda q_: q_.tensor_copy(
            bHrun[:, :], bhp[:, G:2 * G]), deps=[t_h])

        t_pa_pe = S.tok("tensor")
        t_lat_free = t_sp_at_prev
        t_lgx_free = t_sp_gx_prev
        t_bx_free_b = t_bx_free
        t_bit_free = t_bit_r_free
        t_bxf_war = [t_bxf_r_free]
        t_hup = [t_hrun] * G
        t_out_prev = None
        t_up_used = {0: t_pa_pe, 1: t_pa_pe}
        t_dn_used = None
        t_bht_war = None
        t_hid_prev = None
        sp_at_all = ("d_spat", 16 * CH)
        sp_gx_all = ("d_spgx", 16 * CH)

        for q in range(CH):
            t_lat = S.dma("sync", "d_lat", bat3, spat_v[q],
                          deps=[sp_at_all, t_lat_free])
            t_lgx = S.dma("sync", "d_lgx", bgx3, spgx_v[q],
                          deps=[sp_gx_all, t_lgx_free])
            t_lx = S.dma("sync", "d_x", bx3, x_v[q], deps=[t_bx_free_b])

            t_scan = [None] * G
            for g in range(G):
                t_scan[g] = S.op("vector", lambda q_, g=g: q_.tensor_tensor_scan(
                    bit[:, g * tc:(g + 1) * tc], bat[:, g * tc:(g + 1) * tc],
                    bgx[:, g * tc:(g + 1) * tc], bHrun[:, g:g + 1],
                    op0=ALU.mult, op1=ALU.add),
                    deps=[t_lat, t_lgx, t_hup[g], t_bit_free])
                t_hup[g] = S.op("vector", lambda q_, g=g: q_.tensor_copy(
                    bHrun[:, g:g + 1], bit[:, g * tc + tc - 1:g * tc + tc]),
                    deps=[t_scan[g]])
            t_lat_free = t_scan[G - 1]
            t_lgx_free = t_scan[G - 1]

            t_res = []
            for u in range(NT):
                for hb in range(2):
                    bank = pb[hb]
                    for k in range(4):
                        g = hb * 4 + k
                        S.op("tensor", lambda q_, u=u, g=g, k=k, bank=bank:
                             q_.transpose(bank[:, k * 128:(k + 1) * 128],
                                          bit[:, g * tc + u * 128:g * tc + (u + 1) * 128],
                                          bident[:, :]),
                             deps=([t_scan[hb * 4 + kk] for kk in range(4)]
                                   + [t_id, t_evac.get(id(bank))]
                                   if k == 0 else []),
                             inc=(1 if k == 3 else 0))
                    t_tp = S.tok("tensor")
                    t_r = S.op("vector", lambda q_, u=u, hb=hb, bank=bank:
                               q_.tensor_tensor(
                                   bacc[:, u * C + hb * 512:u * C + (hb + 1) * 512],
                                   bank[:, :512],
                                   bx[:, u * C + hb * 512:u * C + (hb + 1) * 512],
                                   op=ALU.add),
                               deps=[t_tp, t_lx, t_out_prev])
                    t_evac[id(bank)] = t_r
                    t_res.append(t_r)
            t_bx_free_b = t_res[-1]

            t_norm, _ = ln_block(
                bacc.ap(), bacc3, bht.ap()[:, :NT * C], bht_uc, bit.ap(),
                deps_in=t_res + [t_bht_war],
                extra_norm_deps=[t_hup[G - 1]])

            t_x2f = tmfm_transpose(bit.ap(), bxf3, t_norm, t_bxf_war)
            t_x2f_all = [t_x2f[u][hb] for u in range(NT) for hb in range(2)]

            t_gate = [None] * NT
            prbank = pb[2]
            for u in range(NT):
                for cg in range(G):
                    deps = []
                    if cg == 0:
                        deps = [t_x2f[u][0], t_x2f[u][1], t_rw,
                                t_evac.get(id(prbank))]
                    S.op("tensor", lambda q_, u=u, cg=cg: q_.matmul(
                        prbank[:, :E],
                        bxf[:, cg * tc + u * 128:cg * tc + (u + 1) * 128],
                        brw[:, cg * E:(cg + 1) * E],
                        start=(cg == 0), stop=(cg == G - 1)),
                        deps=deps, inc=(1 if cg == G - 1 else 0))
                t_rmm = S.tok("tensor")
                t_m1 = S.op("vector", lambda q_, u=u: q_.reduce_max(
                    bnm1[:, u:u + 1], prbank[:, :E], axis=AX.X, negate=True),
                    deps=[t_rmm])
                t_ex = S.op("scalar", lambda q_, u=u: q_.activation(
                    bge[:, u * E:(u + 1) * E], prbank[:, :E], AF.Exp,
                    bias=bnm1[:, u:u + 1]), deps=[t_m1])
                t_mk = S.op("vector", lambda q_, u=u: q_.tensor_scalar(
                    bmsk[:, u * E:(u + 1) * E], prbank[:, :E],
                    bnm1[:, u:u + 1], 0.0, op0=ALU.add, op1=ALU.is_ge),
                    deps=[t_m1])
                t_mk2 = S.op("vector", lambda q_, u=u: q_.scalar_tensor_tensor(
                    bmsk[:, u * E:(u + 1) * E], bmsk[:, u * E:(u + 1) * E],
                    -BIG, prbank[:, :E], op0=ALU.mult, op1=ALU.add),
                    deps=[t_mk])
                t_m2r = S.op("vector", lambda q_, u=u: q_.reduce_max(
                    bnm2[:, u:u + 1], bmsk[:, u * E:(u + 1) * E], axis=AX.X,
                    negate=True), deps=[t_mk2])
                t_se = S.op("vector", lambda q_, u=u: q_.tensor_scalar(
                    bsel[:, u * E:(u + 1) * E], prbank[:, :E],
                    bnm2[:, u:u + 1], 0.0, op0=ALU.add, op1=ALU.is_ge),
                    deps=[t_m2r])
                t_g0 = S.op("vector", lambda q_, u=u: q_.tensor_tensor(
                    bge[:, u * E:(u + 1) * E], bge[:, u * E:(u + 1) * E],
                    bsel[:, u * E:(u + 1) * E], op=ALU.mult), deps=[t_se, t_ex])
                t_evac[id(prbank)] = t_g0
                t_gs = S.op("vector", lambda q_, u=u: q_.reduce_sum(
                    bgs[:, u:u + 1], bge[:, u * E:(u + 1) * E], axis=AX.X),
                    deps=[t_g0])
                t_gr = S.op("vector", lambda q_, u=u: q_.reciprocal(
                    bgr[:, u:u + 1], bgs[:, u:u + 1]), deps=[t_gs])
                t_gate[u] = S.op("vector", lambda q_, u=u: q_.tensor_scalar(
                    bgate[:, u * E:(u + 1) * E], bge[:, u * E:(u + 1) * E],
                    bgr[:, u:u + 1], None, op0=ALU.mult), deps=[t_gr])

            t_acc_last = t_res
            for e in range(E):
                t_upA = S.dma("sync", "d_up0",
                              wUa.ap().rearrange("p (g d1) -> p g d1", d1=C),
                              up_v[e][0], deps=[t_up_used[0]])
                t_upB = S.dma("sync", "d_up1",
                              wUb.ap().rearrange("p (g d1) -> p g d1", d1=C),
                              up_v[e][1], deps=[t_up_used[1]])
                t_hid = [None] * G
                for dk in range(G):
                    for which, bank, wb, t_wl in ((0, pb[3], wUa, t_upA),
                                                  (1, pb[4], wUb, t_upB)):
                        for cg in range(G):
                            deps = []
                            if cg == 0:
                                deps = t_x2f_all + [t_wl, t_evac.get(id(bank))]
                            S.op("tensor", lambda q_, dk=dk, cg=cg, bank=bank, wb=wb:
                                 q_.matmul(
                                     bank[:, :512],
                                     wb[:, cg * C + dk * 128:cg * C + (dk + 1) * 128],
                                     bxf[:, cg * tc:(cg + 1) * tc],
                                     start=(cg == 0), stop=(cg == G - 1)),
                                 deps=deps, inc=(1 if cg == G - 1 else 0))
                        if which == 0:
                            t_h1 = S.tok("tensor")
                        else:
                            t_h2 = S.tok("tensor")
                    t_si = S.op("scalar", lambda q_: q_.activation(
                        bsilu[:, :], pb[3][:, :512], AF.Silu),
                        deps=[t_h1, t_hid_prev])
                    t_evac[id(pb[3])] = t_si
                    t_hid[dk] = S.op("vector", lambda q_, dk=dk: q_.tensor_tensor(
                        bht[:, dk * 512:(dk + 1) * 512], bsilu[:, :],
                        pb[4][:, :512], op=ALU.mult),
                        deps=[t_h2, t_si])
                    t_evac[id(pb[4])] = t_hid[dk]
                    t_hid_prev = t_hid[dk]

                for r in range(2):
                    t_dn = S.dma("sync", "d_dn",
                                 wD.ap().rearrange("p (g h) -> p g h", h=C),
                                 dn_v[e][:, r * 4:(r + 1) * 4, :],
                                 deps=[t_dn_used])
                    t_acc_new = []
                    for u in range(NT):
                        for hc in range(2):
                            bank = pb[6 + hc]
                            for dgl in range(4):
                                dg = r * 4 + dgl
                                deps = []
                                if dgl == 0:
                                    deps = [t_dn, t_evac.get(id(bank))] + t_hid
                                S.op("tensor", lambda q_, u=u, hc=hc, dg=dg,
                                     dgl=dgl, bank=bank:
                                     q_.matmul(
                                         bank[:, :512],
                                         bht[:, dg * 512 + u * 128:dg * 512 + (u + 1) * 128],
                                         wD[:, dgl * C + hc * 512:dgl * C + (hc + 1) * 512],
                                         start=(dgl == 0), stop=(dgl == 3)),
                                     deps=deps, inc=(1 if dgl == 3 else 0))
                            t_dmm = S.tok("tensor")
                            t_a = S.op("vector", lambda q_, u=u, hc=hc, e=e,
                                       bank=bank: q_.scalar_tensor_tensor(
                                           bacc[:, u * C + hc * 512:u * C + (hc + 1) * 512],
                                           bank[:, :512],
                                           bgate[:, u * E + e:u * E + e + 1],
                                           bacc[:, u * C + hc * 512:u * C + (hc + 1) * 512],
                                           op0=ALU.mult, op1=ALU.add),
                                       deps=[t_dmm, t_gate[u],
                                             t_acc_last[u * 2 + hc]])
                            t_evac[id(bank)] = t_a
                            t_acc_new.append(t_a)
                    t_acc_last = t_acc_new
                    t_dn_used = S.tok("tensor")
                t_pe_now = S.tok("tensor")
                t_up_used = {0: t_pe_now, 1: t_pe_now}
                t_bht_war = t_pe_now
                t_bxf_war = [t_pe_now]

            t_out_prev = S.dma("gpsimd", "d_out", out_v[q], bacc3,
                               deps=t_acc_last)
            t_bit_free = t_x2f_all[-1]

        S.fence("sync", [("d_out", 16 * CH), t_out_prev])
        S.build(block)
    return nc


def _stage(inputs, tok):
    x = np.ascontiguousarray(np.asarray(inputs["L0_attn_x"], np.float32)
                             .reshape(-1, C))
    wiT = np.ascontiguousarray(np.asarray(inputs["L0_attn_Wi"], np.float32).T)
    wrT = np.ascontiguousarray(np.asarray(inputs["L0_attn_Wr"], np.float32).T)
    bi = np.ascontiguousarray(np.asarray(inputs["L0_attn_bi"], np.float32)
                              .reshape(G, 128).T)
    br = np.ascontiguousarray(np.asarray(inputs["L0_attn_br"], np.float32)
                              .reshape(G, 128).T)
    lam = np.ascontiguousarray(
        np.tile(np.asarray(inputs["L0_attn_cap_lambda"], np.float32)
                .reshape(1, 1), (128, 1)))
    rw = np.ascontiguousarray(np.asarray(inputs["L0_ffn_router_w"], np.float32).T)
    up = np.ascontiguousarray(np.asarray(inputs["L0_ffn_up_proj"], np.float32))
    dn = np.ascontiguousarray(np.asarray(inputs["L0_ffn_down_proj"], np.float32))
    in_maps = []
    for c in range(NCORES):
        csel = np.ascontiguousarray(
            np.tile(np.eye(NCORES, dtype=np.float32)[c], (128, 1)))
        in_maps.append({
            "x": np.ascontiguousarray(x[c * tok:(c + 1) * tok]),
            "wiT": wiT, "wrT": wrT, "bi": bi, "br": br, "lam": lam,
            "rw": rw, "up": up, "dn": dn, "csel": csel,
        })
    return in_maps


def kernel(**inputs):
    B, T, _ = inputs["L0_attn_x"].shape
    tok = B * T // NCORES
    in_maps = _stage(inputs, tok)
    nc = build_program(tok=tok, tc=512)
    res = run_bass_kernel_spmd(nc, in_maps, core_ids=list(range(NCORES)))
    out = np.concatenate([np.asarray(r["out"]) for r in res.results], axis=0)
    return out.reshape(B, T, C).astype(np.float32)


# version 04
# speedup vs baseline: 1.0781x; 1.0072x over previous
import contextlib
import os
import numpy as np

_DBG = os.environ.get("KDBG", "")

import concourse.bass as bass
import concourse.mybir as mybir
from concourse.bass_utils import run_bass_kernel_spmd

F32 = mybir.dt.float32
BF16 = mybir.dt.bfloat16
AF = mybir.ActivationFunctionType
ALU = mybir.AluOpType
AX = mybir.AxisListType

NCORES = 8
C = 1024
G = 8
E = 8
DFF = 2048
EPS = 1e-5
BIG = 1e9


class Sched:

    ENGS = ("sync", "tensor", "vector", "scalar", "gpsimd")

    def __init__(self, nc, es):
        self.nc = nc
        self.es = es
        self.items = {e: [] for e in self.ENGS}
        self.cnt = {}
        self.sems = {}

    def _sem(self, name):
        if name not in self.sems:
            self.sems[name] = self.es.enter_context(self.nc.semaphore(name))
            self.cnt[name] = 0
        return self.sems[name]

    def op(self, eng, fn, deps=(), inc=1, sem=None):
        if sem is None:
            sem = "e_" + eng
        self._sem(sem)
        waits = {}
        for d in deps:
            if d is None:
                continue
            s, v = d
            self._sem(s)
            if waits.get(s, 0) < v:
                waits[s] = v
        if inc:
            self.cnt[sem] += inc
        self.items[eng].append((fn, waits, sem if inc else None, inc))
        return (sem, self.cnt[sem]) if inc else None

    def tok(self, eng):
        sem = "e_" + eng
        self._sem(sem)
        return (sem, self.cnt[sem])

    def dma(self, eng, sem, out, in_, deps=()):
        def fn(q, out=out, in_=in_):
            return q.dma_start(out=out, in_=in_)
        self._sem(sem)
        deps = list(deps) + [(sem, self.cnt[sem])]
        return self.op(eng, fn, deps=deps, inc=16, sem=sem)

    def fence(self, eng, deps):
        return self.op(eng, lambda q: None, deps=deps, inc=0)

    def build(self, block):
        for eng in self.ENGS:
            items = self.items[eng]
            if not items:
                continue

            def body(q, items=items):
                vclock = {}
                for fn, waits, sem, inc in items:
                    for s in sorted(waits):
                        v = waits[s]
                        if vclock.get(s, 0) < v:
                            q.wait_ge(self.sems[s], v)
                            vclock[s] = v
                    inst = fn(q)
                    if inst is not None and sem is not None:
                        inst.then_inc(self.sems[sem], inc)

            getattr(block, eng)(body)


def build_program(tok=4096, tc=512):
    CH = tok // tc
    NT = tc // 128
    assert NT == 4 and tok % tc == 0

    nc = bass.Bass("TRN2", target_bir_lowering=False, debug=False)

    x_d = nc.dram_tensor("x", [tok, C], F32, kind="ExternalInput").ap()
    wiT_d = nc.dram_tensor("wiT", [C, C], F32, kind="ExternalInput").ap()
    wrT_d = nc.dram_tensor("wrT", [C, C], F32, kind="ExternalInput").ap()
    bi_d = nc.dram_tensor("bi", [128, G], F32, kind="ExternalInput").ap()
    br_d = nc.dram_tensor("br", [128, G], F32, kind="ExternalInput").ap()
    lam_d = nc.dram_tensor("lam", [128, 1], F32, kind="ExternalInput").ap()
    rw_d = nc.dram_tensor("rw", [C, E], F32, kind="ExternalInput").ap()
    up_d = nc.dram_tensor("up", [E, C, DFF], F32, kind="ExternalInput").ap()
    dn_d = nc.dram_tensor("dn", [E, C, C], F32, kind="ExternalInput").ap()
    csel_d = nc.dram_tensor("csel", [128, NCORES], F32, kind="ExternalInput").ap()
    out_d = nc.dram_tensor("out", [tok, C], F32, kind="ExternalOutput").ap()

    sp_at = nc.dram_tensor("sp_at", [CH, G, 128, tc], F32, kind="Internal").ap()
    sp_gx = nc.dram_tensor("sp_gx", [CH, G, 128, tc], F32, kind="Internal").ap()
    agin = nc.dram_tensor("agin", [2, 128, G], F32, kind="Internal")
    agout = nc.dram_tensor("agout", [NCORES, 2, 128, G], F32, kind="Internal")
    sc_uh = nc.dram_tensor("sc_uh", [E, 128, 2 * G * C], BF16, kind="Internal").ap()
    sc_ul = nc.dram_tensor("sc_ul", [E, 128, 2 * G * C], BF16, kind="Internal").ap()
    sc_dh = nc.dram_tensor("sc_dh", [E, 128, G * C], BF16, kind="Internal").ap()
    sc_dl = nc.dram_tensor("sc_dl", [E, 128, G * C], BF16, kind="Internal").ap()

    x_v = x_d.rearrange("(q u p) c -> q p u c", p=128, u=NT)
    out_v = out_d.rearrange("(q u p) c -> q p u c", p=128, u=NT)
    spat_v = sp_at.rearrange("q g p t -> q p g t")
    spgx_v = sp_gx.rearrange("q g p t -> q p g t")
    wiT_v = wiT_d.rearrange("(cg p) d -> p cg d", p=128)
    wrT_v = wrT_d.rearrange("(cg p) d -> p cg d", p=128)
    rw_v = rw_d.rearrange("(cg p) e -> p cg e", p=128)
    up_v = up_d.rearrange("e (cg p) (hh d1) -> e hh p cg d1", p=128, hh=2)
    dn_v = dn_d.rearrange("e (dg p) h -> e p dg h", p=128)
    agout_v = agout.ap().rearrange("r t p g -> p r t g")
    sc_uh_p = sc_uh.rearrange("e p (hh cg d1) -> e hh p cg d1", hh=2, cg=G)
    sc_ul_p = sc_ul.rearrange("e p (hh cg d1) -> e hh p cg d1", hh=2, cg=G)
    sc_dh_p = sc_dh.rearrange("e p (dg h) -> e p dg h", dg=G)
    sc_dl_p = sc_dl.rearrange("e p (dg h) -> e p dg h", dg=G)

    es = contextlib.ExitStack()
    with es:
        bx = nc.alloc_sbuf_tensor("bx", [128, NT * C], F32)
        bxf = nc.alloc_sbuf_tensor("bxf", [128, G * tc], F32)
        bit = nc.alloc_sbuf_tensor("bit", [128, G * tc], F32)
        bat = nc.alloc_sbuf_tensor("bat", [128, G * tc], F32)
        bgx = nc.alloc_sbuf_tensor("bgx", [128, G * tc], F32)
        bacc = nc.alloc_sbuf_tensor("bacc", [128, NT * C], F32)
        wBh = nc.alloc_sbuf_tensor("wBh", [128, 2 * G * C], BF16)
        wBl = nc.alloc_sbuf_tensor("wBl", [128, 2 * G * C], BF16)
        wDh = nc.alloc_sbuf_tensor("wDh", [128, 2 * C], BF16)
        wDl = nc.alloc_sbuf_tensor("wDl", [128, 2 * C], BF16)
        bxh = nc.alloc_sbuf_tensor("bxh", [128, G * tc], BF16)
        bxl = nc.alloc_sbuf_tensor("bxl", [128, G * tc], BF16)
        bhh = nc.alloc_sbuf_tensor("bhh", [128, G * tc], BF16)
        bhl = nc.alloc_sbuf_tensor("bhl", [128, G * tc], BF16)
        pstage = nc.alloc_sbuf_tensor("pstage", [128, C], F32)
        bident = nc.alloc_sbuf_tensor("bident", [128, 128], F32)
        bbi = nc.alloc_sbuf_tensor("bbi", [128, G], F32)
        bbr = nc.alloc_sbuf_tensor("bbr", [128, G], F32)
        brwh = nc.alloc_sbuf_tensor("brwh", [128, G * E], BF16)
        brwl = nc.alloc_sbuf_tensor("brwl", [128, G * E], BF16)
        blam = nc.alloc_sbuf_tensor("blam", [128, 1], F32)
        beps = nc.alloc_sbuf_tensor("beps", [128, 1], F32)
        bone = nc.alloc_sbuf_tensor("bone", [128, 1], F32)
        bcsel = nc.alloc_sbuf_tensor("bcsel", [128, NCORES], F32)
        bsilu = nc.alloc_sbuf_tensor("bsilu", [128, 512], F32)
        bhf = pstage
        bcarL = nc.alloc_sbuf_tensor("bcarL", [128, G], F32)
        bcarP = nc.alloc_sbuf_tensor("bcarP", [128, G], F32)
        bHrun = nc.alloc_sbuf_tensor("bHrun", [128, G], F32)
        bag = bx
        bcand_o, bhp_o, btmp8_o = 128, 192, 208
        bsum = nc.alloc_sbuf_tensor("bsum", [128, NT], F32)
        bssq = nc.alloc_sbuf_tensor("bssq", [128, NT], F32)
        bnm = nc.alloc_sbuf_tensor("bnm", [128, NT], F32)
        bm2t = nc.alloc_sbuf_tensor("bm2t", [128, NT], F32)
        bvar = nc.alloc_sbuf_tensor("bvar", [128, NT], F32)
        bstd = nc.alloc_sbuf_tensor("bstd", [128, NT], F32)
        brstd = nc.alloc_sbuf_tensor("brstd", [128, NT], F32)
        bnm1 = nc.alloc_sbuf_tensor("bnm1", [128, NT], F32)
        bnm2 = nc.alloc_sbuf_tensor("bnm2", [128, NT], F32)
        bgs = nc.alloc_sbuf_tensor("bgs", [128, NT], F32)
        bgr = nc.alloc_sbuf_tensor("bgr", [128, NT], F32)
        bge = nc.alloc_sbuf_tensor("bge", [128, NT * E], F32)
        bmsk = nc.alloc_sbuf_tensor("bmsk", [128, NT * E], F32)
        bsel = bmsk
        bgate = nc.alloc_sbuf_tensor("bgate", [128, NT * E], F32)

        pb = [nc.alloc_psum_tensor("pb%d" % i, [128, 512], F32) for i in range(8)]

        S = Sched(nc, es)
        block = es.enter_context(nc.Block())

        bx3 = bx.ap().rearrange("p (u c) -> p u c", c=C)
        bacc3 = bacc.ap().rearrange("p (u c) -> p u c", c=C)
        bacc_uc = bacc3
        bat3 = bat.ap().rearrange("p (g t) -> p g t", t=tc)
        bgx3 = bgx.ap().rearrange("p (g t) -> p g t", t=tc)
        bxf3 = bxf.ap().rearrange("p (g t) -> p g t", t=tc)
        bxh3 = bxh.ap().rearrange("p (g t) -> p g t", t=tc)
        bxl3 = bxl.ap().rearrange("p (g t) -> p g t", t=tc)
        bxf_uc = bxf.ap().rearrange("p (u c) -> p u c", c=C)

        t_bi = S.dma("sync", "d_w", bbi[:, :], bi_d)
        t_br = S.dma("sync", "d_w", bbr[:, :], br_d)

        t_lam = S.dma("sync", "d_w", blam[:, :], lam_d)
        t_cs = S.dma("sync", "d_w", bcsel[:, :], csel_d)

        t_id0 = S.op("gpsimd", lambda q: q.memset(bident[:, :], 1.0))
        t_id = S.op("gpsimd", lambda q: q.affine_select(
            bident[:, :], bident[:, :], pattern=[[1, 128]],
            compare_op=ALU.is_equal, fill=0.0, channel_multiplier=-1),
            deps=[t_id0])
        t_eps = S.op("vector", lambda q: q.memset(beps[:, :], EPS))
        t_one = S.op("vector", lambda q: q.memset(bone[:, :], 1.0))

        t_c1 = S.op("scalar", lambda q: q.activation(blam[:, :], blam[:, :], AF.Exp),
                    deps=[t_lam])
        t_c2 = S.op("scalar", lambda q: q.activation(blam[:, :], blam[:, :], AF.Ln,
                                                     bias=bone[:, 0:1]),
                    deps=[t_c1, t_one])
        t_c = S.op("scalar", lambda q: q.mul(blam[:, :], blam[:, :], -8.0), deps=[t_c2])

        t_rw = S.dma("sync", "d_ps",
                     pstage.ap()[:, :G * E].rearrange("p (g e) -> p g e", e=E),
                     rw_v)
        t_rwh = S.op("scalar", lambda q: q.activation(
            brwh[:, :], pstage[:, :G * E], AF.Identity), deps=[t_rw])
        t_rwl = S.op("vector", lambda q: q.tensor_tensor(
            brwl[:, :], pstage[:, :G * E], brwh[:, :], op=ALU.subtract),
            deps=[t_rwh])

        t_pin = None
        t_ph = None
        t_pl = None

        def split_piece(src, dst_h, dst_l, in_deps=()):
            nonlocal t_pin, t_ph, t_pl
            t_pin = S.dma("sync", "d_ps", pstage[:, :], src,
                          deps=[t_ph, t_pl] + list(in_deps))
            t_ph = S.op("scalar", lambda q, d=dst_h: q.activation(
                d, pstage[:, :], AF.Identity), deps=[t_pin, t_ph])
            t_pl = S.op("vector", lambda q, dh=dst_h, dl=dst_l: q.tensor_tensor(
                dl, pstage[:, :], dh, op=ALU.subtract), deps=[t_ph, t_pl])
            return t_ph, t_pl

        wBh_v = wBh.ap().rearrange("p (w cg d) -> p w cg d", w=2, cg=G)
        wBl_v = wBl.ap().rearrange("p (w cg d) -> p w cg d", w=2, cg=G)
        t_ph = t_rwh
        t_pl = t_rwl
        t_wsp_all = []
        for wi, wv in ((0, wiT_v), (1, wrT_v)):
            for cg_ in range(G):
                th, tl = split_piece(
                    wv[:, cg_, :],
                    wBh_v[:, wi, cg_, :],
                    wBl_v[:, wi, cg_, :])
                t_wsp_all.append(tl)

        wp_par = [0]
        wp_prev = {0: [None, None], 1: [None, None]}

        def wsplit_one(src, dst_h, dst_l):
            h = wp_par[0] % 2
            wp_par[0] += 1
            th, tl = split_piece(src, wDh[:, h * C:(h + 1) * C],
                                 wDl[:, h * C:(h + 1) * C],
                                 in_deps=wp_prev[h])
            oh = S.dma("gpsimd", "d_po", dst_h, wDh[:, h * C:(h + 1) * C],
                       deps=[th])
            ol = S.dma("gpsimd", "d_po", dst_l, wDl[:, h * C:(h + 1) * C],
                       deps=[tl])
            wp_prev[h] = [oh, ol]
            return ol

        def emit_wsplit(e):
            outs = []
            for hh in range(2):
                for cg_ in range(G):
                    outs.append(wsplit_one(
                        up_v[e][hh][:, cg_, :],
                        sc_uh_p[e][hh][:, cg_, :],
                        sc_ul_p[e][hh][:, cg_, :]))
            for dg_ in range(G):
                outs.append(wsplit_one(
                    dn_v[e][:, dg_, :],
                    sc_dh_p[e][:, dg_, :],
                    sc_dl_p[e][:, dg_, :]))
            return outs[-2:]

        def ln_block(src_ap, src3, sq_flat, sq3, norm_dst, deps_in,
                     extra_norm_deps):
            t_sq = S.op("scalar", lambda q, a=sq_flat, b=src_ap[:, :NT * C]:
                        q.activation(a, b, AF.Square), deps=deps_in)
            t_sums = S.op("vector", lambda q, i=src3: q.reduce_sum(
                bsum[:, :], i, axis=AX.X), deps=deps_in)
            t_ssq = S.op("vector", lambda q, i=sq3: q.reduce_sum(
                bssq[:, :], i, axis=AX.X), deps=[t_sq])
            t_nm = S.op("vector", lambda q: q.tensor_scalar(
                bnm[:, :], bsum[:, :], -1.0 / C, None, op0=ALU.mult), deps=[t_sums])
            t_m2 = S.op("vector", lambda q: q.tensor_tensor(
                bm2t[:, :], bnm[:, :], bnm[:, :], op=ALU.mult), deps=[t_nm])
            t_var = S.op("vector", lambda q: q.scalar_tensor_tensor(
                bvar[:, :], bssq[:, :], 1.0 / C, bm2t[:, :],
                op0=ALU.mult, op1=ALU.subtract), deps=[t_ssq, t_m2])
            t_std = S.op("scalar", lambda q: q.activation(
                bstd[:, :], bvar[:, :], AF.Sqrt, bias=beps[:, 0:1]),
                deps=[t_var, t_eps])
            t_rstd = S.op("vector", lambda q: q.reciprocal(
                brstd[:, :], bstd[:, :]), deps=[t_std])
            t_norm = []
            for u in range(NT):
                t_norm.append(S.op("vector", lambda q, u=u, s=src_ap,
                                   d=norm_dst: q.tensor_scalar(
                                       d[:, u * C:(u + 1) * C],
                                       s[:, u * C:(u + 1) * C],
                                       bnm[:, u:u + 1], brstd[:, u:u + 1],
                                       op0=ALU.add, op1=ALU.mult),
                                   deps=[t_rstd] + list(extra_norm_deps)))
            return t_norm, t_ssq

        t_evac = {}
        t_s_psum = [None]

        def tmfm_split_transpose(src, t_src_ready, dst_war, want_f32):
            t_cp = [[None] * 2 for _ in range(NT)]
            for u in range(NT):
                for hb in range(2):
                    bank = pb[hb]
                    for k in range(4):
                        g = hb * 4 + k
                        S.op("tensor", lambda q, u=u, g=g, k=k, bank=bank, s=src:
                             q.transpose(bank[:, k * 128:(k + 1) * 128],
                                         s[:, u * C + g * 128:u * C + (g + 1) * 128],
                                         bident[:, :]),
                             deps=([t_src_ready[u], t_id, t_evac.get(id(bank)),
                                    t_s_psum[0]]
                                   if k == 0 else []),
                             inc=(1 if k == 3 else 0))
                    t_tp = S.tok("tensor")
                    sl = (slice(None), slice(hb * 4, hb * 4 + 4),
                          slice(u * 128, (u + 1) * 128))
                    bank3 = bank.ap().rearrange("p (k t) -> p k t", t=128)
                    deps_c = [t_tp] + list(dst_war)
                    t_c32 = None
                    if want_f32:
                        t_c32 = S.op("vector", lambda q, d=bxf3[sl], b=bank3:
                                     q.tensor_copy(d, b), deps=deps_c)
                    t_ch = S.op("vector", lambda q, d=bxh3[sl], b=bank3:
                                q.tensor_copy(d, b), deps=deps_c)
                    if want_f32:
                        t_cl = S.op("vector", lambda q, d=bxl3[sl],
                                    s32=bxf3[sl], h=bxh3[sl]:
                                    q.tensor_tensor(
                                        d, s32, h, op=ALU.subtract),
                                    deps=[t_c32, t_ch])
                    else:
                        t_cl = S.op("vector", lambda q, d=bxl3[sl], b=bank3,
                                    h=bxh3[sl]: q.tensor_tensor(
                                        d, b, h, op=ALU.subtract),
                                    deps=[t_ch])
                    t_evac[id(bank)] = t_cl
                    t_cp[u][hb] = t_cl
            return t_cp

        t_bx_free = None
        t_bit_r_free = None
        t_bsplit_war = [t_rwl]
        t_bgx_r_free = None
        t_bacc_scr = None
        t_sp_at_prev = None
        t_sp_gx_prev = None
        t_carL = [None] * G
        t_carP = [None] * G
        t_scr_free = None
        t_split_done = [[] for _ in range(E)]

        for q in range(CH):
            t_lx = S.dma("sync", "d_x", bx3, x_v[q], deps=[t_bx_free])
            t_norm, t_ssq = ln_block(
                bx.ap(), bx3, bacc.ap()[:, :NT * C], bacc_uc, bx.ap(),
                deps_in=[t_lx, t_bacc_scr], extra_norm_deps=[])
            t_bacc_scr = t_ssq
            t_fmcp = tmfm_split_transpose(bx.ap(), t_norm, t_bsplit_war, True)
            t_fm_all = [t_fmcp[u][hb] for u in range(NT) for hb in range(2)]

            t_sig = {0: [None] * G, 1: [None] * G}
            for wi in range(2):
                for dg in range(G):
                    bank = pb[3 + (dg % 2)]
                    passes = (((wBh, bxh), (wBh, bxl), (wBl, bxh))
                              if "no3p" not in _DBG else ((wBh, bxh),))
                    npss = len(passes) - 1
                    for cg in range(G):
                        for ps, (wb, xb) in enumerate(passes):
                            deps = []
                            if cg == 0 and ps == 0:
                                deps = t_fm_all + t_wsp_all + \
                                    [t_evac.get(id(bank))]
                            S.op("tensor", lambda q_, wi=wi, dg=dg, cg=cg,
                                 ps=ps, bank=bank, wb=wb, xb=xb:
                                 q_.matmul(
                                     bank[:, :tc],
                                     wb[:, wi * G * C + cg * C + dg * 128:
                                        wi * G * C + cg * C + (dg + 1) * 128],
                                     xb[:, cg * tc:(cg + 1) * tc],
                                     start=(cg == 0 and ps == 0),
                                     stop=(cg == G - 1 and ps == npss)),
                                 deps=deps,
                                 inc=(1 if (cg == G - 1 and ps == npss) else 0))
                    t_mm = S.tok("tensor")
                    if wi == 0:
                        t_s = S.op("scalar", lambda q_, dg=dg, bank=bank:
                                   q_.activation(
                                       bit[:, dg * tc:(dg + 1) * tc],
                                       bank[:, :tc], AF.Sigmoid,
                                       bias=bbi[:, dg:dg + 1]),
                                   deps=[t_mm, t_bi, t_bit_r_free])
                    else:
                        t_s = S.op("scalar", lambda q_, dg=dg, bank=bank:
                                   q_.activation(
                                       bat[:, dg * tc:(dg + 1) * tc],
                                       bank[:, :tc], AF.Sigmoid,
                                       bias=bbr[:, dg:dg + 1]),
                                   deps=[t_mm, t_br, t_sp_at_prev])
                    t_sig[wi][dg] = t_s
                    t_evac[id(bank)] = t_s
                    t_s_psum[0] = t_s
            t_at = []
            for dg in range(G):
                t_at.append(S.op("scalar", lambda q_, dg=dg: q_.activation(
                    bat[:, dg * tc:(dg + 1) * tc], bat[:, dg * tc:(dg + 1) * tc],
                    AF.Exp, scale=blam[:, 0:1]), deps=[t_sig[1][dg], t_c]))

            t_ixg = [None] * G
            t_ngg = [None] * G
            for dg in range(G):
                t_ixg[dg] = S.op("vector", lambda q_, dg=dg: q_.tensor_tensor(
                    bit[:, dg * tc:(dg + 1) * tc], bit[:, dg * tc:(dg + 1) * tc],
                    bxf[:, dg * tc:(dg + 1) * tc], op=ALU.mult),
                    deps=[t_sig[0][dg]])
                t_ngg[dg] = S.op("vector", lambda q_, dg=dg: q_.scalar_tensor_tensor(
                    bgx[:, dg * tc:(dg + 1) * tc], bat[:, dg * tc:(dg + 1) * tc],
                    -1.0, bat[:, dg * tc:(dg + 1) * tc],
                    op0=ALU.mult, op1=ALU.mult),
                    deps=[t_at[dg], t_sp_gx_prev, t_bgx_r_free])
            t_cfg = [None] * G
            for dg in range(G):
                t_cfg[dg] = S.op("scalar", lambda q_, dg=dg: q_.activation(
                    bgx[:, dg * tc:(dg + 1) * tc], bgx[:, dg * tc:(dg + 1) * tc],
                    AF.Sqrt, bias=bone[:, 0:1]), deps=[t_ngg[dg], t_one])
            t_gxg = [None] * G
            for g in range(G):
                t_gxg[g] = S.op("vector", lambda q_, g=g: q_.tensor_tensor(
                    bgx[:, g * tc:(g + 1) * tc], bgx[:, g * tc:(g + 1) * tc],
                    bit[:, g * tc:(g + 1) * tc], op=ALU.mult),
                    deps=[t_cfg[g], t_ixg[g]])
                initL = 0.0 if q == 0 else bcarL[:, g:g + 1]
                t_ls = S.op("vector", lambda q_, g=g, initL=initL:
                            q_.tensor_tensor_scan(
                                bsilu[:, :], bat[:, g * tc:(g + 1) * tc],
                                bgx[:, g * tc:(g + 1) * tc], initL,
                                op0=ALU.mult, op1=ALU.add),
                            deps=[t_gxg[g], t_carL[g], t_scr_free])
                t_carL[g] = S.op("vector", lambda q_, g=g: q_.tensor_copy(
                    bcarL[:, g:g + 1], bsilu[:, tc - 1:tc]), deps=[t_ls])
                t_scr_free = t_carL[g]
                initP = 1.0 if q == 0 else bcarP[:, g:g + 1]
                t_ps2 = S.op("vector", lambda q_, g=g, initP=initP:
                             q_.tensor_tensor_scan(
                                 bsilu[:, :], bat[:, g * tc:(g + 1) * tc],
                                 bat[:, g * tc:(g + 1) * tc], initP,
                                 op0=ALU.mult, op1=ALU.bypass),
                             deps=[t_carP[g], t_scr_free, t_at[g]])
                t_carP[g] = S.op("vector", lambda q_, g=g: q_.tensor_copy(
                    bcarP[:, g:g + 1], bsilu[:, tc - 1:tc]), deps=[t_ps2])
                t_scr_free = t_carP[g]

            t_sp_at_prev = S.dma("gpsimd", "d_spat", spat_v[q], bat3, deps=t_at)
            t_sp_gx_prev = S.dma("gpsimd", "d_spgx", spgx_v[q], bgx3,
                                 deps=list(t_gxg))
            t_bit_r_free = t_gxg[G - 1]
            t_bsplit_war = [S.tok("tensor")]
            t_bgx_r_free = t_carP[G - 1]
            t_bx_free = t_fm_all[-1]

            if "nosplit" not in _DBG:
                for e_ in range(E):
                    if e_ * CH // E == q:
                        t_split_done[e_] = emit_wsplit(e_)

        t_e1 = S.dma("gpsimd", "d_ag", agin.ap()[0], bcarP[:, :],
                     deps=[t_carP[g] for g in range(G)])
        t_e2 = S.dma("gpsimd", "d_ag", agin.ap()[1], bcarL[:, :],
                     deps=[t_carL[g] for g in range(G)])
        t_cc = S.op("gpsimd", lambda q_: q_.collective_compute(
            "AllGather", ALU.bypass, replica_groups=[list(range(NCORES))],
            ins=[agin.ap().opt()], outs=[agout.ap().opt()]),
            deps=[t_e1, t_e2], sem="s_cc")
        t_bag = S.dma("sync", "d_bag",
                      bag.ap()[:, :NCORES * 2 * G].rearrange(
                          "p (r t g) -> p r t g", t=2, g=G),
                      agout_v, deps=[t_cc, t_bx_free])

        def cnd(i):
            return bag[:, bcand_o + i * G:bcand_o + (i + 1) * G]

        t_cand = [None] * NCORES
        t_cand[0] = S.op("vector", lambda q_: q_.memset(cnd(0), 0.0),
                         deps=[t_bag])
        t_cand[4] = S.op("vector", lambda q_: q_.memset(cnd(4), 0.0),
                         deps=[t_bag])
        t_btmp_r = None
        for b in range(2):
            for k in range(1, 4):
                i = 4 * b + k
                r = 4 * b + k - 1
                tm = S.op("vector", lambda q_, i=i, r=r: q_.tensor_tensor(
                    bag[:, btmp8_o:btmp8_o + G], bag[:, r * 16:r * 16 + G],
                    cnd(i - 1), op=ALU.mult),
                    deps=[t_bag, t_cand[i - 1], t_btmp_r])
                t_cand[i] = S.op("vector", lambda q_, i=i, r=r: q_.tensor_tensor(
                    cnd(i), bag[:, r * 16 + G:r * 16 + 2 * G],
                    bag[:, btmp8_o:btmp8_o + G], op=ALU.add), deps=[tm])
                t_btmp_r = t_cand[i]
        t_h = S.op("vector", lambda q_: q_.tensor_scalar(
            bag[:, bhp_o:bhp_o + G], cnd(0), bcsel[:, 0:1], None, op0=ALU.mult),
            deps=[t_cand[0], t_cs])
        for i in range(1, NCORES):
            src = ((i + 1) % 2) * G
            dst = (i % 2) * G
            t_h = S.op("vector", lambda q_, i=i, src=src, dst=dst:
                       q_.scalar_tensor_tensor(
                           bag[:, bhp_o + dst:bhp_o + dst + G], cnd(i),
                           bcsel[:, i:i + 1], bag[:, bhp_o + src:bhp_o + src + G],
                           op0=ALU.mult, op1=ALU.add),
                       deps=[t_cand[i], t_h])
        t_hrun = S.op("vector", lambda q_: q_.tensor_copy(
            bHrun[:, :], bag[:, bhp_o + G:bhp_o + 2 * G]), deps=[t_h])

        t_pa_pe = S.tok("tensor")
        t_lat_free = t_sp_at_prev
        t_lgx_free = t_sp_gx_prev
        t_bx_free_b = t_bx_free
        t_bit_free = t_bit_r_free
        t_bxf_scr = t_bit_r_free
        t_hup = [t_hrun] * G
        t_out_prev = None
        t_up_used = {0: t_pa_pe, 1: t_pa_pe}
        t_dn_used = [t_ph, t_pl, ("d_po", S.cnt.get("d_po", 0))]
        t_hid_prev = t_scr_free
        t_bhid_war = None
        t_bhf_war = t_pl
        sp_at_all = ("d_spat", 16 * CH)
        sp_gx_all = ("d_spgx", 16 * CH)

        for q in range(CH):
            t_lat = S.dma("sync", "d_lat", bat3, spat_v[q],
                          deps=[sp_at_all, t_lat_free])
            t_lgx = S.dma("sync", "d_lgx", bgx3, spgx_v[q],
                          deps=[sp_gx_all, t_lgx_free])
            t_lx = S.dma("sync", "d_x", bx3, x_v[q],
                         deps=[t_bx_free_b, t_hrun if q == 0 else None])

            t_scan = [None] * G
            for g in range(G):
                t_scan[g] = S.op("vector", lambda q_, g=g: q_.tensor_tensor_scan(
                    bit[:, g * tc:(g + 1) * tc], bat[:, g * tc:(g + 1) * tc],
                    bgx[:, g * tc:(g + 1) * tc], bHrun[:, g:g + 1],
                    op0=ALU.mult, op1=ALU.add),
                    deps=[t_lat, t_lgx, t_hup[g], t_bit_free])
                t_hup[g] = S.op("vector", lambda q_, g=g: q_.tensor_copy(
                    bHrun[:, g:g + 1], bit[:, g * tc + tc - 1:g * tc + tc]),
                    deps=[t_scan[g]])
            t_lat_free = t_scan[G - 1]
            t_lgx_free = t_scan[G - 1]

            t_res = []
            for u in range(NT):
                for hb in range(2):
                    bank = pb[hb]
                    for k in range(4):
                        g = hb * 4 + k
                        S.op("tensor", lambda q_, u=u, g=g, k=k, bank=bank:
                             q_.transpose(bank[:, k * 128:(k + 1) * 128],
                                          bit[:, g * tc + u * 128:g * tc + (u + 1) * 128],
                                          bident[:, :]),
                             deps=([t_scan[hb * 4 + kk] for kk in range(4)]
                                   + [t_id, t_evac.get(id(bank)), t_s_psum[0]]
                                   if k == 0 else []),
                             inc=(1 if k == 3 else 0))
                    t_tp = S.tok("tensor")
                    t_r = S.op("vector", lambda q_, u=u, hb=hb, bank=bank:
                               q_.tensor_tensor(
                                   bacc[:, u * C + hb * 512:u * C + (hb + 1) * 512],
                                   bank[:, :512],
                                   bx[:, u * C + hb * 512:u * C + (hb + 1) * 512],
                                   op=ALU.add),
                               deps=[t_tp, t_lx, t_out_prev])
                    t_evac[id(bank)] = t_r
                    t_res.append(t_r)
            t_bx_free_b = t_res[-1]

            t_norm, t_ssq2 = ln_block(
                bacc.ap(), bacc3, bxf.ap()[:, :NT * C], bxf_uc, bit.ap(),
                deps_in=t_res + [t_bxf_scr],
                extra_norm_deps=[t_hup[G - 1]])
            t_bxf_scr = t_ssq2

            t_x2f = tmfm_split_transpose(bit.ap(), t_norm, t_bsplit_war, False)
            t_x2f_all = [t_x2f[u][hb] for u in range(NT) for hb in range(2)]

            t_gate = [None] * NT
            prbank = pb[2]
            for u in range(NT if "norouter" not in _DBG else 0):
                for cg in range(G):
                    for ps, (xb, rb) in enumerate(
                            ((bxh, brwh), (bxl, brwh), (bxh, brwl))):
                        deps = []
                        if cg == 0 and ps == 0:
                            deps = [t_x2f[u][0], t_x2f[u][1], t_rwl,
                                    t_evac.get(id(prbank))]
                        S.op("tensor", lambda q_, u=u, cg=cg, ps=ps, xb=xb, rb=rb:
                             q_.matmul(
                                 prbank[:, :E],
                                 xb[:, cg * tc + u * 128:cg * tc + (u + 1) * 128],
                                 rb[:, cg * E:(cg + 1) * E],
                                 start=(cg == 0 and ps == 0),
                                 stop=(cg == G - 1 and ps == 2)),
                             deps=deps,
                             inc=(1 if (cg == G - 1 and ps == 2) else 0))
                t_rmm = S.tok("tensor")
                t_m1 = S.op("vector", lambda q_, u=u: q_.reduce_max(
                    bnm1[:, u:u + 1], prbank[:, :E], axis=AX.X, negate=True),
                    deps=[t_rmm])
                t_ex = S.op("scalar", lambda q_, u=u: q_.activation(
                    bge[:, u * E:(u + 1) * E], prbank[:, :E], AF.Exp,
                    bias=bnm1[:, u:u + 1]), deps=[t_m1, t_x2f_all[-1]])
                t_s_psum[0] = t_ex
                t_mk = S.op("vector", lambda q_, u=u: q_.tensor_scalar(
                    bmsk[:, u * E:(u + 1) * E], prbank[:, :E],
                    bnm1[:, u:u + 1], 0.0, op0=ALU.add, op1=ALU.is_ge),
                    deps=[t_m1])
                t_mk2 = S.op("vector", lambda q_, u=u: q_.scalar_tensor_tensor(
                    bmsk[:, u * E:(u + 1) * E], bmsk[:, u * E:(u + 1) * E],
                    -BIG, prbank[:, :E], op0=ALU.mult, op1=ALU.add),
                    deps=[t_mk])
                t_m2r = S.op("vector", lambda q_, u=u: q_.reduce_max(
                    bnm2[:, u:u + 1], bmsk[:, u * E:(u + 1) * E], axis=AX.X,
                    negate=True), deps=[t_mk2])
                t_se = S.op("vector", lambda q_, u=u: q_.tensor_scalar(
                    bsel[:, u * E:(u + 1) * E], prbank[:, :E],
                    bnm2[:, u:u + 1], 0.0, op0=ALU.add, op1=ALU.is_ge),
                    deps=[t_m2r])
                t_g0 = S.op("vector", lambda q_, u=u: q_.tensor_tensor(
                    bge[:, u * E:(u + 1) * E], bge[:, u * E:(u + 1) * E],
                    bsel[:, u * E:(u + 1) * E], op=ALU.mult), deps=[t_se, t_ex])
                t_evac[id(prbank)] = t_g0
                t_gs = S.op("vector", lambda q_, u=u: q_.reduce_sum(
                    bgs[:, u:u + 1], bge[:, u * E:(u + 1) * E], axis=AX.X),
                    deps=[t_g0])
                t_gr = S.op("vector", lambda q_, u=u: q_.reciprocal(
                    bgr[:, u:u + 1], bgs[:, u:u + 1]), deps=[t_gs])
                t_gate[u] = S.op("vector", lambda q_, u=u: q_.tensor_scalar(
                    bgate[:, u * E:(u + 1) * E], bge[:, u * E:(u + 1) * E],
                    bgr[:, u:u + 1], None, op0=ALU.mult), deps=[t_gr])

            t_acc_last = t_res
            for e in range(E if "noffn" not in _DBG else 0):
                sd = t_split_done[e]
                t_upA = S.dma("sync", "d_uph", wBh[:, :], sc_uh[e],
                              deps=[t_up_used[0]] + sd)
                t_upB = S.dma("sync", "d_upl", wBl[:, :], sc_ul[e],
                              deps=[t_up_used[1]] + sd)
                t_hid = [None] * G
                for dk in range(G):
                    for which, bank in ((0, pb[3]), (1, pb[4])):
                        for cg in range(G):
                            for ps, (wb, xb) in enumerate(
                                    ((wBh, bxh), (wBh, bxl), (wBl, bxh))):
                                deps = []
                                if cg == 0 and ps == 0:
                                    deps = t_x2f_all + [t_upA, t_upB,
                                                       t_evac.get(id(bank))]
                                S.op("tensor", lambda q_, dk=dk, cg=cg, ps=ps,
                                     which=which, bank=bank, wb=wb, xb=xb:
                                     q_.matmul(
                                         bank[:, :512],
                                         wb[:, which * G * C + cg * C + dk * 128:
                                            which * G * C + cg * C + (dk + 1) * 128],
                                         xb[:, cg * tc:(cg + 1) * tc],
                                         start=(cg == 0 and ps == 0),
                                         stop=(cg == G - 1 and ps == 2)),
                                     deps=deps,
                                     inc=(1 if (cg == G - 1 and ps == 2) else 0))
                        if which == 0:
                            t_h1 = S.tok("tensor")
                        else:
                            t_h2 = S.tok("tensor")
                    t_si = S.op("scalar", lambda q_: q_.activation(
                        bsilu[:, :], pb[3][:, :512], AF.Silu),
                        deps=[t_h1, t_hid_prev])
                    t_evac[id(pb[3])] = t_si
                    t_s_psum[0] = t_si
                    t_tt = S.op("vector", lambda q_: q_.tensor_tensor(
                        bhf[:, :512], bsilu[:, :], pb[4][:, :512], op=ALU.mult),
                        deps=[t_h2, t_si, t_bhf_war])
                    t_evac[id(pb[4])] = t_tt
                    t_hh2 = S.op("scalar", lambda q_, dk=dk: q_.activation(
                        bhh[:, dk * 512:(dk + 1) * 512], bhf[:, :512],
                        AF.Identity), deps=[t_tt, t_bhid_war])
                    t_hl2 = S.op("vector", lambda q_, dk=dk: q_.tensor_tensor(
                        bhl[:, dk * 512:(dk + 1) * 512], bhf[:, :512],
                        bhh[:, dk * 512:(dk + 1) * 512], op=ALU.subtract),
                        deps=[t_hh2])
                    t_bhf_war = t_hl2
                    t_hid[dk] = t_hl2
                    t_hid_prev = t_tt

                t_dmm = {}
                for r in range(4):
                    t_dh = S.dma("sync", "d_dnh",
                                 wDh.ap().rearrange("p (k d) -> p k d", k=2),
                                 sc_dh_p[e][:, 2 * r:2 * r + 2, :],
                                 deps=t_dn_used)
                    t_dl = S.dma("sync", "d_dnl",
                                 wDl.ap().rearrange("p (k d) -> p k d", k=2),
                                 sc_dl_p[e][:, 2 * r:2 * r + 2, :],
                                 deps=t_dn_used)
                    for u in range(NT):
                        for hc in range(2):
                            bank = pb[u * 2 + hc]
                            for dgl in range(2):
                                dg = 2 * r + dgl
                                for ps, (hb_, db_) in enumerate(
                                        ((bhh, wDh), (bhh, wDl), (bhl, wDh))):
                                    deps = []
                                    if r == 0 and dgl == 0 and ps == 0:
                                        deps = [t_dh, t_dl,
                                                t_evac.get(id(bank))] + t_hid
                                    elif dgl == 0 and ps == 0 and u == 0 and hc == 0:
                                        deps = [t_dh, t_dl]
                                    S.op("tensor", lambda q_, u=u, hc=hc, dg=dg,
                                         dgl=dgl, r=r, ps=ps, bank=bank,
                                         hb_=hb_, db_=db_:
                                         q_.matmul(
                                             bank[:, :512],
                                             hb_[:, dg * 512 + u * 128:
                                                 dg * 512 + (u + 1) * 128],
                                             db_[:, dgl * C + hc * 512:
                                                 dgl * C + (hc + 1) * 512],
                                             start=(r == 0 and dgl == 0 and ps == 0),
                                             stop=(r == 3 and dgl == 1 and ps == 2)),
                                         deps=deps,
                                         inc=(1 if (dgl == 1 and ps == 2
                                                    and ((u == NT - 1 and hc == 1)
                                                         or r == 3))
                                              else 0))
                                if r == 3 and dgl == 1:
                                    t_dmm[(u, hc)] = S.tok("tensor")
                    t_dn_used = [S.tok("tensor")]
                t_acc_new = [None] * 8
                for u, hc in ((1, 1), (2, 0), (2, 1), (0, 0), (0, 1), (1, 0),
                              (3, 0), (3, 1)):
                    bank = pb[u * 2 + hc]
                    t_a = S.op("vector", lambda q_, u=u, hc=hc, e=e,
                               bank=bank: q_.scalar_tensor_tensor(
                                   bacc[:, u * C + hc * 512:u * C + (hc + 1) * 512],
                                   bank[:, :512],
                                   bgate[:, u * E + e:u * E + e + 1],
                                   bacc[:, u * C + hc * 512:u * C + (hc + 1) * 512],
                                   op0=ALU.mult, op1=ALU.add),
                               deps=[t_dmm[(u, hc)], t_gate[u],
                                     t_acc_last[u * 2 + hc]])
                    t_evac[id(bank)] = t_a
                    t_acc_new[u * 2 + hc] = t_a
                t_acc_last = t_acc_new
                t_pe_now = S.tok("tensor")
                t_up_used = {0: t_pe_now, 1: t_pe_now}
                t_bhid_war = t_pe_now
                t_bsplit_war = [t_pe_now]

            t_out_prev = S.dma("gpsimd", "d_out", out_v[q], bacc3,
                               deps=t_acc_last)
            t_bit_free = t_x2f_all[-1]

        S.fence("sync", [("d_out", 16 * CH), t_out_prev])
        S.build(block)
    return nc


def _stage(inputs, tok):
    x = np.ascontiguousarray(np.asarray(inputs["L0_attn_x"], np.float32)
                             .reshape(-1, C))
    wiT = np.ascontiguousarray(np.asarray(inputs["L0_attn_Wi"], np.float32).T)
    wrT = np.ascontiguousarray(np.asarray(inputs["L0_attn_Wr"], np.float32).T)
    bi = np.ascontiguousarray(np.asarray(inputs["L0_attn_bi"], np.float32)
                              .reshape(G, 128).T)
    br = np.ascontiguousarray(np.asarray(inputs["L0_attn_br"], np.float32)
                              .reshape(G, 128).T)
    lam = np.ascontiguousarray(
        np.tile(np.asarray(inputs["L0_attn_cap_lambda"], np.float32)
                .reshape(1, 1), (128, 1)))
    rw = np.ascontiguousarray(np.asarray(inputs["L0_ffn_router_w"], np.float32).T)
    up = np.ascontiguousarray(np.asarray(inputs["L0_ffn_up_proj"], np.float32))
    dn = np.ascontiguousarray(np.asarray(inputs["L0_ffn_down_proj"], np.float32))
    in_maps = []
    for c in range(NCORES):
        csel = np.ascontiguousarray(
            np.tile(np.eye(NCORES, dtype=np.float32)[c], (128, 1)))
        in_maps.append({
            "x": np.ascontiguousarray(x[c * tok:(c + 1) * tok]),
            "wiT": wiT, "wrT": wrT, "bi": bi, "br": br, "lam": lam,
            "rw": rw, "up": up, "dn": dn, "csel": csel,
        })
    return in_maps


def kernel(**inputs):
    B, T, _ = inputs["L0_attn_x"].shape
    tok = B * T // NCORES
    in_maps = _stage(inputs, tok)
    nc = build_program(tok=tok, tc=512)
    res = run_bass_kernel_spmd(nc, in_maps, core_ids=list(range(NCORES)))
    out = np.concatenate([np.asarray(r["out"]) for r in res.results], axis=0)
    return out.reshape(B, T, C).astype(np.float32)


# version 05
# speedup vs baseline: 1.1142x; 1.0335x over previous
import contextlib
import os
import numpy as np

_DBG = os.environ.get("KDBG", "")

import concourse.bass as bass
import concourse.mybir as mybir
from concourse.bass_utils import run_bass_kernel_spmd

F32 = mybir.dt.float32
BF16 = mybir.dt.bfloat16
AF = mybir.ActivationFunctionType
ALU = mybir.AluOpType
AX = mybir.AxisListType

NCORES = 8
C = 1024
G = 8
E = 8
DFF = 2048
EPS = 1e-5
BIG = 1e9


class Sched:

    ENGS = ("sync", "tensor", "vector", "scalar", "gpsimd")

    def __init__(self, nc, es):
        self.nc = nc
        self.es = es
        self.items = {e: [] for e in self.ENGS}
        self.cnt = {}
        self.sems = {}
        self.rot = {}

    def _sem(self, name):
        if name not in self.sems:
            self.sems[name] = self.es.enter_context(self.nc.semaphore(name))
            self.cnt[name] = 0
        return self.sems[name]

    def op(self, eng, fn, deps=(), inc=1, sem=None):
        if sem is None:
            sem = "e_" + eng
        self._sem(sem)
        waits = {}
        for d in deps:
            if d is None:
                continue
            s, v = d
            self._sem(s)
            if waits.get(s, 0) < v:
                waits[s] = v
        if inc:
            self.cnt[sem] += inc
        self.items[eng].append((fn, waits, sem if inc else None, inc))
        return (sem, self.cnt[sem]) if inc else None

    def tok(self, eng):
        sem = "e_" + eng
        self._sem(sem)
        return (sem, self.cnt[sem])

    ROT = ()

    def dma(self, eng, sem, out, in_, deps=()):
        def fn(q, out=out, in_=in_):
            return q.dma_start(out=out, in_=in_)
        if sem in self.ROT:
            n = self.rot.get(sem, 0)
            self.rot[sem] = n + 1
            sem = "%s_%d" % (sem, n // 14)
        self._sem(sem)
        deps = list(deps) + [(sem, self.cnt[sem])]
        return self.op(eng, fn, deps=deps, inc=16, sem=sem)

    def fence(self, eng, deps):
        return self.op(eng, lambda q: None, deps=deps, inc=0)

    def build(self, block):
        for eng in self.ENGS:
            items = self.items[eng]
            if not items:
                continue

            def body(q, items=items):
                vclock = {}
                for fn, waits, sem, inc in items:
                    for s in sorted(waits):
                        v = waits[s]
                        if vclock.get(s, 0) < v:
                            q.wait_ge(self.sems[s], v)
                            vclock[s] = v
                    inst = fn(q)
                    if inst is not None and sem is not None:
                        inst.then_inc(self.sems[sem], inc)

            getattr(block, eng)(body)


def build_program(tok=4096, tc=512):
    CH = tok // tc
    NT = tc // 128
    assert NT == 4 and tok % tc == 0

    nc = bass.Bass("TRN2", target_bir_lowering=False, debug=False)

    x_d = nc.dram_tensor("x", [tok, C], F32, kind="ExternalInput").ap()
    wiT_d = nc.dram_tensor("wiT", [C, C], F32, kind="ExternalInput").ap()
    wrT_d = nc.dram_tensor("wrT", [C, C], F32, kind="ExternalInput").ap()
    bi_d = nc.dram_tensor("bi", [128, G], F32, kind="ExternalInput").ap()
    br_d = nc.dram_tensor("br", [128, G], F32, kind="ExternalInput").ap()
    lam_d = nc.dram_tensor("lam", [128, 1], F32, kind="ExternalInput").ap()
    rw_d = nc.dram_tensor("rw", [C, E], F32, kind="ExternalInput").ap()
    up_d = nc.dram_tensor("up", [E, C, DFF], F32, kind="ExternalInput").ap()
    dn_d = nc.dram_tensor("dn", [E, C, C], F32, kind="ExternalInput").ap()
    csel_d = nc.dram_tensor("csel", [128, NCORES], F32, kind="ExternalInput").ap()
    out_d = nc.dram_tensor("out", [tok, C], F32, kind="ExternalOutput").ap()

    sp_at = nc.dram_tensor("sp_at", [CH, G, 128, tc], F32, kind="Internal").ap()
    sp_gx = nc.dram_tensor("sp_gx", [CH, G, 128, tc], F32, kind="Internal").ap()
    agin = nc.dram_tensor("agin", [2, 128, G], F32, kind="Internal")
    agout = nc.dram_tensor("agout", [NCORES, 2, 128, G], F32, kind="Internal")
    sc_uh = nc.dram_tensor("sc_uh", [E, 128, 2 * G * C], BF16, kind="Internal").ap()
    sc_ul = nc.dram_tensor("sc_ul", [E, 128, 2 * G * C], BF16, kind="Internal").ap()
    sc_dh = nc.dram_tensor("sc_dh", [E, 128, G * C], BF16, kind="Internal").ap()
    sc_dl = nc.dram_tensor("sc_dl", [E, 128, G * C], BF16, kind="Internal").ap()

    x_v = x_d.rearrange("(q u p) c -> q p u c", p=128, u=NT)
    out_v = out_d.rearrange("(q u p) c -> q p u c", p=128, u=NT)
    spat_v = sp_at.rearrange("q g p t -> q p g t")
    spgx_v = sp_gx.rearrange("q g p t -> q p g t")
    wiT_v = wiT_d.rearrange("(cg p) d -> p cg d", p=128)
    wrT_v = wrT_d.rearrange("(cg p) d -> p cg d", p=128)
    rw_v = rw_d.rearrange("(cg p) e -> p cg e", p=128)
    up_v = up_d.rearrange("e (cg p) (hh d1) -> e hh p cg d1", p=128, hh=2)
    dn_v = dn_d.rearrange("e (dg p) h -> e p dg h", p=128)
    agout_v = agout.ap().rearrange("r t p g -> p r t g")
    sc_uh_p = sc_uh.rearrange("e p (hh cg d1) -> e hh p cg d1", hh=2, cg=G)
    sc_ul_p = sc_ul.rearrange("e p (hh cg d1) -> e hh p cg d1", hh=2, cg=G)
    sc_dh_p = sc_dh.rearrange("e p (dg h) -> e p dg h", dg=G)
    sc_dl_p = sc_dl.rearrange("e p (dg h) -> e p dg h", dg=G)

    es = contextlib.ExitStack()
    with es:
        bx = nc.alloc_sbuf_tensor("bx", [128, NT * C], F32)
        bxf = nc.alloc_sbuf_tensor("bxf", [128, G * tc], F32)
        bit = nc.alloc_sbuf_tensor("bit", [128, G * tc], F32)
        bat = nc.alloc_sbuf_tensor("bat", [128, G * tc], F32)
        bgx = nc.alloc_sbuf_tensor("bgx", [128, G * tc], F32)
        bacc = nc.alloc_sbuf_tensor("bacc", [128, NT * C], F32)
        wBh = nc.alloc_sbuf_tensor("wBh", [128, 2 * G * C], BF16)
        wBl = nc.alloc_sbuf_tensor("wBl", [128, 2 * G * C], BF16)
        wDh = nc.alloc_sbuf_tensor("wDh", [128, 2 * C], BF16)
        wDl = nc.alloc_sbuf_tensor("wDl", [128, 2 * C], BF16)
        bxh = nc.alloc_sbuf_tensor("bxh", [128, G * tc], BF16)
        bxl = nc.alloc_sbuf_tensor("bxl", [128, G * tc], BF16)
        bhh = nc.alloc_sbuf_tensor("bhh", [128, G * tc], BF16)
        bhl = nc.alloc_sbuf_tensor("bhl", [128, G * tc], BF16)
        pstage = nc.alloc_sbuf_tensor("pstage", [128, C], F32)
        bident = nc.alloc_sbuf_tensor("bident", [128, 128], F32)
        bbi = nc.alloc_sbuf_tensor("bbi", [128, G], F32)
        bbr = nc.alloc_sbuf_tensor("bbr", [128, G], F32)
        brwh = nc.alloc_sbuf_tensor("brwh", [128, G * E], BF16)
        brwl = nc.alloc_sbuf_tensor("brwl", [128, G * E], BF16)
        blam = nc.alloc_sbuf_tensor("blam", [128, 1], F32)
        beps = nc.alloc_sbuf_tensor("beps", [128, 1], F32)
        bone = nc.alloc_sbuf_tensor("bone", [128, 1], F32)
        bcsel = nc.alloc_sbuf_tensor("bcsel", [128, NCORES], F32)
        bsilu = nc.alloc_sbuf_tensor("bsilu", [128, 512], F32)
        bhf = bat
        bcarL = nc.alloc_sbuf_tensor("bcarL", [128, G], F32)
        bcarP = nc.alloc_sbuf_tensor("bcarP", [128, G], F32)
        bHrun = nc.alloc_sbuf_tensor("bHrun", [128, G], F32)
        bag = bx
        bcand_o, bhp_o, btmp8_o = 128, 192, 208
        bsum = nc.alloc_sbuf_tensor("bsum", [128, NT], F32)
        bssq = nc.alloc_sbuf_tensor("bssq", [128, NT], F32)
        bnm = nc.alloc_sbuf_tensor("bnm", [128, NT], F32)
        bm2t = nc.alloc_sbuf_tensor("bm2t", [128, NT], F32)
        bvar = nc.alloc_sbuf_tensor("bvar", [128, NT], F32)
        bstd = nc.alloc_sbuf_tensor("bstd", [128, NT], F32)
        brstd = nc.alloc_sbuf_tensor("brstd", [128, NT], F32)
        bnm1 = nc.alloc_sbuf_tensor("bnm1", [128, NT], F32)
        bnm2 = nc.alloc_sbuf_tensor("bnm2", [128, NT], F32)
        bgs = nc.alloc_sbuf_tensor("bgs", [128, NT], F32)
        bgr = nc.alloc_sbuf_tensor("bgr", [128, NT], F32)
        bge = nc.alloc_sbuf_tensor("bge", [128, NT * E], F32)
        bmsk = nc.alloc_sbuf_tensor("bmsk", [128, NT * E], F32)
        bsel = bmsk
        bgate = nc.alloc_sbuf_tensor("bgate", [128, NT * E], F32)

        pb = [nc.alloc_psum_tensor("pb%d" % i, [128, 512], F32) for i in range(8)]

        S = Sched(nc, es)
        block = es.enter_context(nc.Block())

        bx3 = bx.ap().rearrange("p (u c) -> p u c", c=C)
        bacc3 = bacc.ap().rearrange("p (u c) -> p u c", c=C)
        bacc_uc = bacc3
        bat3 = bat.ap().rearrange("p (g t) -> p g t", t=tc)
        bgx3 = bgx.ap().rearrange("p (g t) -> p g t", t=tc)
        bxf3 = bxf.ap().rearrange("p (g t) -> p g t", t=tc)
        bxh3 = bxh.ap().rearrange("p (g t) -> p g t", t=tc)
        bxl3 = bxl.ap().rearrange("p (g t) -> p g t", t=tc)
        bxf_uc = bxf.ap().rearrange("p (u c) -> p u c", c=C)

        t_bi = S.dma("sync", "d_w", bbi[:, :], bi_d)
        t_br = S.dma("sync", "d_w", bbr[:, :], br_d)

        t_lam = S.dma("sync", "d_w", blam[:, :], lam_d)
        t_cs = S.dma("sync", "d_w", bcsel[:, :], csel_d)

        t_id0 = S.op("gpsimd", lambda q: q.memset(bident[:, :], 1.0))
        t_id = S.op("gpsimd", lambda q: q.affine_select(
            bident[:, :], bident[:, :], pattern=[[1, 128]],
            compare_op=ALU.is_equal, fill=0.0, channel_multiplier=-1),
            deps=[t_id0])
        t_eps = S.op("vector", lambda q: q.memset(beps[:, :], EPS))
        t_one = S.op("vector", lambda q: q.memset(bone[:, :], 1.0))

        t_c1 = S.op("scalar", lambda q: q.activation(blam[:, :], blam[:, :], AF.Exp),
                    deps=[t_lam])
        t_c2 = S.op("scalar", lambda q: q.activation(blam[:, :], blam[:, :], AF.Ln,
                                                     bias=bone[:, 0:1]),
                    deps=[t_c1, t_one])
        t_c = S.op("scalar", lambda q: q.mul(blam[:, :], blam[:, :], -8.0), deps=[t_c2])

        t_rw = S.dma("sync", "d_w",
                     pstage.ap()[:, :G * E].rearrange("p (g e) -> p g e", e=E),
                     rw_v)
        t_rwh = S.op("scalar", lambda q: q.activation(
            brwh[:, :], pstage[:, :G * E], AF.Identity), deps=[t_rw])
        t_rwl = S.op("vector", lambda q: q.tensor_tensor(
            brwl[:, :], pstage[:, :G * E], brwh[:, :], op=ALU.subtract),
            deps=[t_rwh])

        t_pin = None
        t_ph = None
        t_pl = None

        def split_piece(src, dst_h, dst_l, in_deps=()):
            nonlocal t_pin, t_ph, t_pl
            t_pin = S.dma("sync", "d_ps", pstage[:, :], src,
                          deps=[t_ph, t_pl] + list(in_deps))
            t_ph = S.op("scalar", lambda q, d=dst_h: q.activation(
                d, pstage[:, :], AF.Identity), deps=[t_pin, t_ph])
            t_pl = S.op("vector", lambda q, dh=dst_h, dl=dst_l: q.tensor_tensor(
                dl, pstage[:, :], dh, op=ALU.subtract), deps=[t_ph, t_pl])
            return t_ph, t_pl

        wBh_v = wBh.ap().rearrange("p (w cg d) -> p w cg d", w=2, cg=G)
        wBl_v = wBl.ap().rearrange("p (w cg d) -> p w cg d", w=2, cg=G)
        t_ph = t_rwh
        t_pl = t_rwl
        t_wsp_all = []
        for wi, wv in ((0, wiT_v), (1, wrT_v)):
            for cg_ in range(G):
                th, tl = split_piece(
                    wv[:, cg_, :],
                    wBh_v[:, wi, cg_, :],
                    wBl_v[:, wi, cg_, :])
                t_wsp_all.append(tl)

        t_hi_cast = {}
        hi_jobs = []
        for e_ in range(E):
            hi_jobs.append((e_, 0, sc_uh_p[e_][0], up_v[e_][0]))
            hi_jobs.append((e_, 1, sc_uh_p[e_][1], up_v[e_][1]))
            hi_jobs.append((e_, 2, sc_dh_p[e_], dn_v[e_]))
        hi_pos = [0]

        def pump_hi(n=1):
            for _ in range(n):
                if hi_pos[0] >= len(hi_jobs):
                    return
                e_, k_, dst, src = hi_jobs[hi_pos[0]]
                hi_pos[0] += 1
                t_hi_cast.setdefault(e_, [None] * 3)[k_] = S.dma(
                    "gpsimd", "d_hi", dst, src)

        wp_par = [0]
        wp_prev = {0: [t_ph, t_pl], 1: [t_ph, t_pl]}

        def wsplit_one(src, src_h, dst_l, hi_dep):
            h = wp_par[0] % 2
            wp_par[0] += 1
            hi_sl = wDh[:, h * C:(h + 1) * C]
            lo_sl = wDl[:, h * C:(h + 1) * C]
            t_pin = S.dma("scalar", "d_ps", pstage[:, :1024], src,
                          deps=wp_prev[h] + [wp_prev[1 - h][0]])
            t_hin = S.dma("scalar", "d_ph", hi_sl, src_h,
                          deps=wp_prev[h] + [hi_dep])
            t_sub = S.op("gpsimd", lambda q, hs=hi_sl, ls=lo_sl: q.tensor_tensor(
                ls, pstage[:, :1024], hs, op=ALU.subtract),
                deps=[t_pin, t_hin])
            ol = S.dma("scalar", "d_po", dst_l, lo_sl, deps=[t_sub])
            wp_prev[h] = [t_sub, ol]
            return ol

        def emit_wsplit(e):
            while hi_pos[0] <= 3 * e + 2:
                pump_hi()
            outs = []
            n_ = 0
            for hh in range(2):
                for cg_ in range(G):
                    outs.append(wsplit_one(
                        up_v[e][hh][:, cg_, :],
                        sc_uh_p[e][hh][:, cg_, :],
                        sc_ul_p[e][hh][:, cg_, :],
                        t_hi_cast[e][hh]))
                    n_ += 1
                    if n_ % 8 == 0:
                        pump_hi()
            for dg_ in range(G):
                outs.append(wsplit_one(
                    dn_v[e][:, dg_, :],
                    sc_dh_p[e][:, dg_, :],
                    sc_dl_p[e][:, dg_, :],
                    t_hi_cast[e][2]))
                n_ += 1
                if n_ % 8 == 0:
                    pump_hi()
            return outs[-2:]

        t_split_done = [[] for _ in range(E)]
        for e_ in range(5):
            t_split_done[e_] = emit_wsplit(e_)

        def ln_block(src_ap, src3, sq_flat, sq3, norm_dst, deps_in,
                     extra_norm_deps):
            t_sq = S.op("scalar", lambda q, a=sq_flat, b=src_ap[:, :NT * C]:
                        q.activation(a, b, AF.Square), deps=deps_in)
            t_sums = S.op("vector", lambda q, i=src3: q.reduce_sum(
                bsum[:, :], i, axis=AX.X), deps=deps_in)
            t_ssq = S.op("vector", lambda q, i=sq3: q.reduce_sum(
                bssq[:, :], i, axis=AX.X), deps=[t_sq])
            t_nm = S.op("vector", lambda q: q.tensor_scalar(
                bnm[:, :], bsum[:, :], -1.0 / C, None, op0=ALU.mult), deps=[t_sums])
            t_m2 = S.op("vector", lambda q: q.tensor_tensor(
                bm2t[:, :], bnm[:, :], bnm[:, :], op=ALU.mult), deps=[t_nm])
            t_var = S.op("vector", lambda q: q.scalar_tensor_tensor(
                bvar[:, :], bssq[:, :], 1.0 / C, bm2t[:, :],
                op0=ALU.mult, op1=ALU.subtract), deps=[t_ssq, t_m2])
            t_std = S.op("scalar", lambda q: q.activation(
                bstd[:, :], bvar[:, :], AF.Sqrt, bias=beps[:, 0:1]),
                deps=[t_var, t_eps])
            t_rstd = S.op("vector", lambda q: q.reciprocal(
                brstd[:, :], bstd[:, :]), deps=[t_std])
            t_norm = []
            for u in range(NT):
                t_norm.append(S.op("vector", lambda q, u=u, s=src_ap,
                                   d=norm_dst: q.tensor_scalar(
                                       d[:, u * C:(u + 1) * C],
                                       s[:, u * C:(u + 1) * C],
                                       bnm[:, u:u + 1], brstd[:, u:u + 1],
                                       op0=ALU.add, op1=ALU.mult),
                                   deps=[t_rstd] + list(extra_norm_deps)))
            return t_norm, t_ssq

        t_evac = {}
        t_s_psum = [None]

        def tmfm_split_transpose(src, t_src_ready, dst_war, want_f32):
            t_cp = [[None] * 2 for _ in range(NT)]
            for u in range(NT):
                for hb in range(2):
                    bank = pb[hb]
                    for k in range(4):
                        g = hb * 4 + k
                        S.op("tensor", lambda q, u=u, g=g, k=k, bank=bank, s=src:
                             q.transpose(bank[:, k * 128:(k + 1) * 128],
                                         s[:, u * C + g * 128:u * C + (g + 1) * 128],
                                         bident[:, :]),
                             deps=([t_src_ready[u], t_id, t_evac.get(id(bank)),
                                    t_s_psum[0]]
                                   if k == 0 else []),
                             inc=(1 if k == 3 else 0))
                    t_tp = S.tok("tensor")
                    sl = (slice(None), slice(hb * 4, hb * 4 + 4),
                          slice(u * 128, (u + 1) * 128))
                    bank3 = bank.ap().rearrange("p (k t) -> p k t", t=128)
                    deps_c = [t_tp] + list(dst_war)
                    t_c32 = None
                    if want_f32:
                        t_c32 = S.op("vector", lambda q, d=bxf3[sl], b=bank3:
                                     q.tensor_copy(d, b), deps=deps_c)
                    t_ch = S.op("vector", lambda q, d=bxh3[sl], b=bank3:
                                q.tensor_copy(d, b), deps=deps_c)
                    if want_f32:
                        t_cl = S.op("vector", lambda q, d=bxl3[sl],
                                    s32=bxf3[sl], h=bxh3[sl]:
                                    q.tensor_tensor(
                                        d, s32, h, op=ALU.subtract),
                                    deps=[t_c32, t_ch])
                    else:
                        t_cl = S.op("vector", lambda q, d=bxl3[sl], b=bank3,
                                    h=bxh3[sl]: q.tensor_tensor(
                                        d, b, h, op=ALU.subtract),
                                    deps=[t_ch])
                    t_evac[id(bank)] = t_cl
                    t_cp[u][hb] = t_cl
            return t_cp

        t_bx_free = None
        t_bit_r_free = None
        t_bsplit_war = [t_rwl]
        t_bgx_r_free = None
        t_bacc_scr = None
        t_sp_at_prev = None
        t_sp_gx_prev = None
        t_carL = [None] * G
        t_carP = [None] * G
        t_scr_free = None

        for q in range(CH):
            t_lx = S.dma("sync", "d_x", bx3, x_v[q], deps=[t_bx_free])
            t_norm, t_ssq = ln_block(
                bx.ap(), bx3, bacc.ap()[:, :NT * C], bacc_uc, bx.ap(),
                deps_in=[t_lx, t_bacc_scr], extra_norm_deps=[])
            t_bacc_scr = t_ssq
            t_fmcp = tmfm_split_transpose(bx.ap(), t_norm, t_bsplit_war, True)
            t_fm_all = [t_fmcp[u][hb] for u in range(NT) for hb in range(2)]

            t_sig = {0: [None] * G, 1: [None] * G}
            for wi in range(2):
                for dg in range(G):
                    bank = pb[3 + (dg % 2)]
                    passes = (((wBh, bxh), (wBh, bxl), (wBl, bxh))
                              if "no3p" not in _DBG else ((wBh, bxh),))
                    npss = len(passes) - 1
                    for cg in range(G):
                        for ps, (wb, xb) in enumerate(passes):
                            deps = []
                            if cg == 0 and ps == 0:
                                deps = t_fm_all + t_wsp_all + \
                                    [t_evac.get(id(bank))]
                            S.op("tensor", lambda q_, wi=wi, dg=dg, cg=cg,
                                 ps=ps, bank=bank, wb=wb, xb=xb:
                                 q_.matmul(
                                     bank[:, :tc],
                                     wb[:, wi * G * C + cg * C + dg * 128:
                                        wi * G * C + cg * C + (dg + 1) * 128],
                                     xb[:, cg * tc:(cg + 1) * tc],
                                     start=(cg == 0 and ps == 0),
                                     stop=(cg == G - 1 and ps == npss)),
                                 deps=deps,
                                 inc=(1 if (cg == G - 1 and ps == npss) else 0))
                    t_mm = S.tok("tensor")
                    if wi == 0:
                        t_s = S.op("scalar", lambda q_, dg=dg, bank=bank:
                                   q_.activation(
                                       bit[:, dg * tc:(dg + 1) * tc],
                                       bank[:, :tc], AF.Sigmoid,
                                       bias=bbi[:, dg:dg + 1]),
                                   deps=[t_mm, t_bi, t_bit_r_free])
                    else:
                        t_s = S.op("scalar", lambda q_, dg=dg, bank=bank:
                                   q_.activation(
                                       bat[:, dg * tc:(dg + 1) * tc],
                                       bank[:, :tc], AF.Sigmoid,
                                       bias=bbr[:, dg:dg + 1]),
                                   deps=[t_mm, t_br, t_sp_at_prev])
                    t_sig[wi][dg] = t_s
                    t_evac[id(bank)] = t_s
                    t_s_psum[0] = t_s
            t_at = []
            for dg in range(G):
                t_at.append(S.op("scalar", lambda q_, dg=dg: q_.activation(
                    bat[:, dg * tc:(dg + 1) * tc], bat[:, dg * tc:(dg + 1) * tc],
                    AF.Exp, scale=blam[:, 0:1]), deps=[t_sig[1][dg], t_c]))

            t_ixg = [None] * G
            t_ngg = [None] * G
            for dg in range(G):
                t_ixg[dg] = S.op("vector", lambda q_, dg=dg: q_.tensor_tensor(
                    bit[:, dg * tc:(dg + 1) * tc], bit[:, dg * tc:(dg + 1) * tc],
                    bxf[:, dg * tc:(dg + 1) * tc], op=ALU.mult),
                    deps=[t_sig[0][dg]])
                t_ngg[dg] = S.op("vector", lambda q_, dg=dg: q_.scalar_tensor_tensor(
                    bgx[:, dg * tc:(dg + 1) * tc], bat[:, dg * tc:(dg + 1) * tc],
                    -1.0, bat[:, dg * tc:(dg + 1) * tc],
                    op0=ALU.mult, op1=ALU.mult),
                    deps=[t_at[dg], t_sp_gx_prev, t_bgx_r_free])
            t_cfg = [None] * G
            for dg in range(G):
                t_cfg[dg] = S.op("scalar", lambda q_, dg=dg: q_.activation(
                    bgx[:, dg * tc:(dg + 1) * tc], bgx[:, dg * tc:(dg + 1) * tc],
                    AF.Sqrt, bias=bone[:, 0:1]), deps=[t_ngg[dg], t_one])
            t_gxg = [None] * G
            for g in range(G):
                t_gxg[g] = S.op("vector", lambda q_, g=g: q_.tensor_tensor(
                    bgx[:, g * tc:(g + 1) * tc], bgx[:, g * tc:(g + 1) * tc],
                    bit[:, g * tc:(g + 1) * tc], op=ALU.mult),
                    deps=[t_cfg[g], t_ixg[g]])
                initL = 0.0 if q == 0 else bcarL[:, g:g + 1]
                t_ls = S.op("vector", lambda q_, g=g, initL=initL:
                            q_.tensor_tensor_scan(
                                bsilu[:, :], bat[:, g * tc:(g + 1) * tc],
                                bgx[:, g * tc:(g + 1) * tc], initL,
                                op0=ALU.mult, op1=ALU.add),
                            deps=[t_gxg[g], t_carL[g], t_scr_free])
                t_carL[g] = S.op("vector", lambda q_, g=g: q_.tensor_copy(
                    bcarL[:, g:g + 1], bsilu[:, tc - 1:tc]), deps=[t_ls])
                t_scr_free = t_carL[g]
                initP = 1.0 if q == 0 else bcarP[:, g:g + 1]
                t_ps2 = S.op("vector", lambda q_, g=g, initP=initP:
                             q_.tensor_tensor_scan(
                                 bsilu[:, :], bat[:, g * tc:(g + 1) * tc],
                                 bat[:, g * tc:(g + 1) * tc], initP,
                                 op0=ALU.mult, op1=ALU.bypass),
                             deps=[t_carP[g], t_scr_free, t_at[g]])
                t_carP[g] = S.op("vector", lambda q_, g=g: q_.tensor_copy(
                    bcarP[:, g:g + 1], bsilu[:, tc - 1:tc]), deps=[t_ps2])
                t_scr_free = t_carP[g]

            t_sp_at_prev = S.dma("sync", "d_spat", spat_v[q], bat3, deps=t_at)
            t_sp_gx_prev = S.dma("sync", "d_spgx", spgx_v[q], bgx3,
                                 deps=list(t_gxg))
            t_bit_r_free = t_gxg[G - 1]
            t_bsplit_war = [S.tok("tensor")]
            t_bgx_r_free = t_carP[G - 1]
            t_bx_free = t_fm_all[-1]



        t_e1 = S.dma("sync", "d_ag", agin.ap()[0], bcarP[:, :],
                     deps=[t_carP[g] for g in range(G)])
        t_e2 = S.dma("sync", "d_ag", agin.ap()[1], bcarL[:, :],
                     deps=[t_carL[g] for g in range(G)])
        t_cc = S.op("gpsimd", lambda q_: q_.collective_compute(
            "AllGather", ALU.bypass, replica_groups=[list(range(NCORES))],
            ins=[agin.ap().opt()], outs=[agout.ap().opt()]),
            deps=[t_e1, t_e2], sem="s_cc")
        t_bag = S.dma("sync", "d_bag",
                      bag.ap()[:, :NCORES * 2 * G].rearrange(
                          "p (r t g) -> p r t g", t=2, g=G),
                      agout_v, deps=[t_cc, t_bx_free])

        def cnd(i):
            return bag[:, bcand_o + i * G:bcand_o + (i + 1) * G]

        t_cand = [None] * NCORES
        t_cand[0] = S.op("vector", lambda q_: q_.memset(cnd(0), 0.0),
                         deps=[t_bag])
        t_cand[4] = S.op("vector", lambda q_: q_.memset(cnd(4), 0.0),
                         deps=[t_bag])
        t_btmp_r = None
        for b in range(2):
            for k in range(1, 4):
                i = 4 * b + k
                r = 4 * b + k - 1
                tm = S.op("vector", lambda q_, i=i, r=r: q_.tensor_tensor(
                    bag[:, btmp8_o:btmp8_o + G], bag[:, r * 16:r * 16 + G],
                    cnd(i - 1), op=ALU.mult),
                    deps=[t_bag, t_cand[i - 1], t_btmp_r])
                t_cand[i] = S.op("vector", lambda q_, i=i, r=r: q_.tensor_tensor(
                    cnd(i), bag[:, r * 16 + G:r * 16 + 2 * G],
                    bag[:, btmp8_o:btmp8_o + G], op=ALU.add), deps=[tm])
                t_btmp_r = t_cand[i]
        t_h = S.op("vector", lambda q_: q_.tensor_scalar(
            bag[:, bhp_o:bhp_o + G], cnd(0), bcsel[:, 0:1], None, op0=ALU.mult),
            deps=[t_cand[0], t_cs])
        for i in range(1, NCORES):
            src = ((i + 1) % 2) * G
            dst = (i % 2) * G
            t_h = S.op("vector", lambda q_, i=i, src=src, dst=dst:
                       q_.scalar_tensor_tensor(
                           bag[:, bhp_o + dst:bhp_o + dst + G], cnd(i),
                           bcsel[:, i:i + 1], bag[:, bhp_o + src:bhp_o + src + G],
                           op0=ALU.mult, op1=ALU.add),
                       deps=[t_cand[i], t_h])
        t_hrun = S.op("vector", lambda q_: q_.tensor_copy(
            bHrun[:, :], bag[:, bhp_o + G:bhp_o + 2 * G]), deps=[t_h])

        for e_ in range(5, E):
            t_split_done[e_] = emit_wsplit(e_)

        t_pa_pe = S.tok("tensor")
        t_lat_free = t_sp_at_prev
        t_lgx_free = t_sp_gx_prev
        t_bx_free_b = t_bx_free
        t_bit_free = t_bit_r_free
        t_bxf_scr = t_bit_r_free
        t_hup = [t_hrun] * G
        t_out_prev = None
        t_up_used = {0: t_pa_pe, 1: t_pa_pe}
        t_dn_used = wp_prev[0] + wp_prev[1]
        t_hid_prev = t_scr_free
        t_bhid_war = None
        t_bhf_war = t_sp_at_prev
        sp_at_all = ("d_spat", 16 * CH)
        sp_gx_all = ("d_spgx", 16 * CH)

        for q in range(CH):
            t_lat = S.dma("sync", "d_lat", bat3, spat_v[q],
                          deps=[sp_at_all, t_lat_free])
            t_lgx = S.dma("sync", "d_lgx", bgx3, spgx_v[q],
                          deps=[sp_gx_all, t_lgx_free])
            t_lx = S.dma("sync", "d_x", bx3, x_v[q],
                         deps=[t_bx_free_b, t_hrun if q == 0 else None])

            t_scan = [None] * G
            for g in range(G):
                t_scan[g] = S.op("vector", lambda q_, g=g: q_.tensor_tensor_scan(
                    bit[:, g * tc:(g + 1) * tc], bat[:, g * tc:(g + 1) * tc],
                    bgx[:, g * tc:(g + 1) * tc], bHrun[:, g:g + 1],
                    op0=ALU.mult, op1=ALU.add),
                    deps=[t_lat, t_lgx, t_hup[g], t_bit_free])
                t_hup[g] = S.op("vector", lambda q_, g=g: q_.tensor_copy(
                    bHrun[:, g:g + 1], bit[:, g * tc + tc - 1:g * tc + tc]),
                    deps=[t_scan[g]])
            t_lat_free = t_scan[G - 1]
            t_lgx_free = t_scan[G - 1]

            t_res = []
            for u in range(NT):
                for hb in range(2):
                    bank = pb[hb]
                    for k in range(4):
                        g = hb * 4 + k
                        S.op("tensor", lambda q_, u=u, g=g, k=k, bank=bank:
                             q_.transpose(bank[:, k * 128:(k + 1) * 128],
                                          bit[:, g * tc + u * 128:g * tc + (u + 1) * 128],
                                          bident[:, :]),
                             deps=([t_scan[hb * 4 + kk] for kk in range(4)]
                                   + [t_id, t_evac.get(id(bank)), t_s_psum[0]]
                                   if k == 0 else []),
                             inc=(1 if k == 3 else 0))
                    t_tp = S.tok("tensor")
                    t_r = S.op("vector", lambda q_, u=u, hb=hb, bank=bank:
                               q_.tensor_tensor(
                                   bacc[:, u * C + hb * 512:u * C + (hb + 1) * 512],
                                   bank[:, :512],
                                   bx[:, u * C + hb * 512:u * C + (hb + 1) * 512],
                                   op=ALU.add),
                               deps=[t_tp, t_lx, t_out_prev])
                    t_evac[id(bank)] = t_r
                    t_res.append(t_r)
            t_bx_free_b = t_res[-1]

            t_norm, t_ssq2 = ln_block(
                bacc.ap(), bacc3, bxf.ap()[:, :NT * C], bxf_uc, bit.ap(),
                deps_in=t_res + [t_bxf_scr],
                extra_norm_deps=[t_hup[G - 1]])
            t_bxf_scr = t_ssq2

            t_x2f = tmfm_split_transpose(bit.ap(), t_norm, t_bsplit_war, False)
            t_x2f_all = [t_x2f[u][hb] for u in range(NT) for hb in range(2)]

            t_gate = [None] * NT
            prbank = pb[2]
            for u in range(NT if "norouter" not in _DBG else 0):
                for cg in range(G):
                    for ps, (xb, rb) in enumerate(
                            ((bxh, brwh), (bxl, brwh), (bxh, brwl))):
                        deps = []
                        if cg == 0 and ps == 0:
                            deps = [t_x2f[u][0], t_x2f[u][1], t_rwl,
                                    t_evac.get(id(prbank))]
                        S.op("tensor", lambda q_, u=u, cg=cg, ps=ps, xb=xb, rb=rb:
                             q_.matmul(
                                 prbank[:, :E],
                                 xb[:, cg * tc + u * 128:cg * tc + (u + 1) * 128],
                                 rb[:, cg * E:(cg + 1) * E],
                                 start=(cg == 0 and ps == 0),
                                 stop=(cg == G - 1 and ps == 2)),
                             deps=deps,
                             inc=(1 if (cg == G - 1 and ps == 2) else 0))
                t_rmm = S.tok("tensor")
                t_m1 = S.op("vector", lambda q_, u=u: q_.reduce_max(
                    bnm1[:, u:u + 1], prbank[:, :E], axis=AX.X, negate=True),
                    deps=[t_rmm])
                t_ex = S.op("scalar", lambda q_, u=u: q_.activation(
                    bge[:, u * E:(u + 1) * E], prbank[:, :E], AF.Exp,
                    bias=bnm1[:, u:u + 1]), deps=[t_m1, t_x2f_all[-1]])
                t_s_psum[0] = t_ex
                t_mk = S.op("vector", lambda q_, u=u: q_.tensor_scalar(
                    bmsk[:, u * E:(u + 1) * E], prbank[:, :E],
                    bnm1[:, u:u + 1], 0.0, op0=ALU.add, op1=ALU.is_ge),
                    deps=[t_m1])
                t_mk2 = S.op("vector", lambda q_, u=u: q_.scalar_tensor_tensor(
                    bmsk[:, u * E:(u + 1) * E], bmsk[:, u * E:(u + 1) * E],
                    -BIG, prbank[:, :E], op0=ALU.mult, op1=ALU.add),
                    deps=[t_mk])
                t_m2r = S.op("vector", lambda q_, u=u: q_.reduce_max(
                    bnm2[:, u:u + 1], bmsk[:, u * E:(u + 1) * E], axis=AX.X,
                    negate=True), deps=[t_mk2])
                t_se = S.op("vector", lambda q_, u=u: q_.tensor_scalar(
                    bsel[:, u * E:(u + 1) * E], prbank[:, :E],
                    bnm2[:, u:u + 1], 0.0, op0=ALU.add, op1=ALU.is_ge),
                    deps=[t_m2r])
                t_g0 = S.op("vector", lambda q_, u=u: q_.tensor_tensor(
                    bge[:, u * E:(u + 1) * E], bge[:, u * E:(u + 1) * E],
                    bsel[:, u * E:(u + 1) * E], op=ALU.mult), deps=[t_se, t_ex])
                t_evac[id(prbank)] = t_g0
                t_gs = S.op("vector", lambda q_, u=u: q_.reduce_sum(
                    bgs[:, u:u + 1], bge[:, u * E:(u + 1) * E], axis=AX.X),
                    deps=[t_g0])
                t_gr = S.op("vector", lambda q_, u=u: q_.reciprocal(
                    bgr[:, u:u + 1], bgs[:, u:u + 1]), deps=[t_gs])
                t_gate[u] = S.op("vector", lambda q_, u=u: q_.tensor_scalar(
                    bgate[:, u * E:(u + 1) * E], bge[:, u * E:(u + 1) * E],
                    bgr[:, u:u + 1], None, op0=ALU.mult), deps=[t_gr])

            t_acc_last = t_res
            for e in range(E if "noffn" not in _DBG else 0):
                sd = t_split_done[e]
                t_upA = S.dma("sync", "d_uph", wBh[:, :], sc_uh[e],
                              deps=[t_up_used[0]] + sd)
                t_upB = S.dma("sync", "d_upl", wBl[:, :], sc_ul[e],
                              deps=[t_up_used[1]] + sd)
                t_hid = [None] * G
                for dk in range(G):
                    for which, bank in ((0, pb[3]), (1, pb[4])):
                        for cg in range(G):
                            for ps, (wb, xb) in enumerate(
                                    ((wBh, bxh), (wBh, bxl), (wBl, bxh))):
                                deps = []
                                if cg == 0 and ps == 0:
                                    deps = t_x2f_all + [t_upA, t_upB,
                                                       t_evac.get(id(bank))]
                                S.op("tensor", lambda q_, dk=dk, cg=cg, ps=ps,
                                     which=which, bank=bank, wb=wb, xb=xb:
                                     q_.matmul(
                                         bank[:, :512],
                                         wb[:, which * G * C + cg * C + dk * 128:
                                            which * G * C + cg * C + (dk + 1) * 128],
                                         xb[:, cg * tc:(cg + 1) * tc],
                                         start=(cg == 0 and ps == 0),
                                         stop=(cg == G - 1 and ps == 2)),
                                     deps=deps,
                                     inc=(1 if (cg == G - 1 and ps == 2) else 0))
                        if which == 0:
                            t_h1 = S.tok("tensor")
                        else:
                            t_h2 = S.tok("tensor")
                    t_si = S.op("scalar", lambda q_: q_.activation(
                        bsilu[:, :], pb[3][:, :512], AF.Silu),
                        deps=[t_h1, t_hid_prev])
                    t_evac[id(pb[3])] = t_si
                    t_s_psum[0] = t_si
                    t_tt = S.op("vector", lambda q_: q_.tensor_tensor(
                        bhf[:, :512], bsilu[:, :], pb[4][:, :512], op=ALU.mult),
                        deps=[t_h2, t_si, t_bhf_war, t_scan[G - 1]])
                    t_evac[id(pb[4])] = t_tt
                    t_hh2 = S.op("scalar", lambda q_, dk=dk: q_.activation(
                        bhh[:, dk * 512:(dk + 1) * 512], bhf[:, :512],
                        AF.Identity), deps=[t_tt, t_bhid_war])
                    t_hl2 = S.op("vector", lambda q_, dk=dk: q_.tensor_tensor(
                        bhl[:, dk * 512:(dk + 1) * 512], bhf[:, :512],
                        bhh[:, dk * 512:(dk + 1) * 512], op=ALU.subtract),
                        deps=[t_hh2])
                    t_bhf_war = t_hl2
                    t_hid[dk] = t_hl2
                    t_hid_prev = t_tt
                    t_bhf_war = t_hl2

                t_up_done = S.tok("tensor")

                t_dmm = {}
                for r in range(4):
                    t_dh = S.dma("sync", "d_dnh",
                                 wDh.ap().rearrange("p (k d) -> p k d", k=2),
                                 sc_dh_p[e][:, 2 * r:2 * r + 2, :],
                                 deps=t_dn_used)
                    t_dl = S.dma("sync", "d_dnl",
                                 wDl.ap().rearrange("p (k d) -> p k d", k=2),
                                 sc_dl_p[e][:, 2 * r:2 * r + 2, :],
                                 deps=t_dn_used)
                    for u in range(NT):
                        for hc in range(2):
                            bank = pb[u * 2 + hc]
                            for dgl in range(2):
                                dg = 2 * r + dgl
                                for ps, (hb_, db_) in enumerate(
                                        ((bhh, wDh), (bhh, wDl), (bhl, wDh))):
                                    deps = []
                                    if r == 0 and dgl == 0 and ps == 0:
                                        deps = [t_dh, t_dl,
                                                t_evac.get(id(bank))] + t_hid
                                    elif dgl == 0 and ps == 0 and u == 0 and hc == 0:
                                        deps = [t_dh, t_dl]
                                    S.op("tensor", lambda q_, u=u, hc=hc, dg=dg,
                                         dgl=dgl, r=r, ps=ps, bank=bank,
                                         hb_=hb_, db_=db_:
                                         q_.matmul(
                                             bank[:, :512],
                                             hb_[:, dg * 512 + u * 128:
                                                 dg * 512 + (u + 1) * 128],
                                             db_[:, dgl * C + hc * 512:
                                                 dgl * C + (hc + 1) * 512],
                                             start=(r == 0 and dgl == 0 and ps == 0),
                                             stop=(r == 3 and dgl == 1 and ps == 2)),
                                         deps=deps,
                                         inc=(1 if (dgl == 1 and ps == 2
                                                    and ((u == NT - 1 and hc == 1)
                                                         or r == 3))
                                              else 0))
                                if r == 3 and dgl == 1:
                                    t_dmm[(u, hc)] = S.tok("tensor")
                    t_dn_used = [S.tok("tensor")]
                t_acc_new = [None] * 8
                for u, hc in ((1, 1), (2, 0), (2, 1), (0, 0), (0, 1), (1, 0),
                              (3, 0), (3, 1)):
                    bank = pb[u * 2 + hc]
                    t_a = S.op("vector", lambda q_, u=u, hc=hc, e=e,
                               bank=bank: q_.scalar_tensor_tensor(
                                   bacc[:, u * C + hc * 512:u * C + (hc + 1) * 512],
                                   bank[:, :512],
                                   bgate[:, u * E + e:u * E + e + 1],
                                   bacc[:, u * C + hc * 512:u * C + (hc + 1) * 512],
                                   op0=ALU.mult, op1=ALU.add),
                               deps=[t_dmm[(u, hc)], t_gate[u],
                                     t_acc_last[u * 2 + hc]])
                    t_evac[id(bank)] = t_a
                    t_acc_new[u * 2 + hc] = t_a
                t_acc_last = t_acc_new
                t_pe_now = S.tok("tensor")
                t_up_used = {0: t_up_done, 1: t_up_done}
                t_bhid_war = t_pe_now
                t_bsplit_war = [t_pe_now]

            t_out_prev = S.dma("sync", "d_out", out_v[q], bacc3,
                               deps=t_acc_last)
            t_bit_free = t_x2f_all[-1]
            t_lat_free = t_bhf_war

        S.fence("sync", [("d_out", 16 * CH), t_out_prev])
        S.build(block)
    return nc


def _stage(inputs, tok):
    x = np.ascontiguousarray(np.asarray(inputs["L0_attn_x"], np.float32)
                             .reshape(-1, C))
    wiT = np.ascontiguousarray(np.asarray(inputs["L0_attn_Wi"], np.float32).T)
    wrT = np.ascontiguousarray(np.asarray(inputs["L0_attn_Wr"], np.float32).T)
    bi = np.ascontiguousarray(np.asarray(inputs["L0_attn_bi"], np.float32)
                              .reshape(G, 128).T)
    br = np.ascontiguousarray(np.asarray(inputs["L0_attn_br"], np.float32)
                              .reshape(G, 128).T)
    lam = np.ascontiguousarray(
        np.tile(np.asarray(inputs["L0_attn_cap_lambda"], np.float32)
                .reshape(1, 1), (128, 1)))
    rw = np.ascontiguousarray(np.asarray(inputs["L0_ffn_router_w"], np.float32).T)
    up = np.ascontiguousarray(np.asarray(inputs["L0_ffn_up_proj"], np.float32))
    dn = np.ascontiguousarray(np.asarray(inputs["L0_ffn_down_proj"], np.float32))
    in_maps = []
    for c in range(NCORES):
        csel = np.ascontiguousarray(
            np.tile(np.eye(NCORES, dtype=np.float32)[c], (128, 1)))
        in_maps.append({
            "x": np.ascontiguousarray(x[c * tok:(c + 1) * tok]),
            "wiT": wiT, "wrT": wrT, "bi": bi, "br": br, "lam": lam,
            "rw": rw, "up": up, "dn": dn, "csel": csel,
        })
    return in_maps


def kernel(**inputs):
    B, T, _ = inputs["L0_attn_x"].shape
    tok = B * T // NCORES
    in_maps = _stage(inputs, tok)
    nc = build_program(tok=tok, tc=512)
    res = run_bass_kernel_spmd(nc, in_maps, core_ids=list(range(NCORES)))
    out = np.concatenate([np.asarray(r["out"]) for r in res.results], axis=0)
    return out.reshape(B, T, C).astype(np.float32)
